# version 02
# baseline (speedup 1.0000x reference)
import numpy as np
from contextlib import ExitStack
import concourse.bass as bass
import concourse.mybir as mybir
from concourse.bass_utils import run_bass_kernel_spmd

F32 = mybir.dt.float32
BF16 = mybir.dt.bfloat16
AF = mybir.ActivationFunctionType
ALU = mybir.AluOpType
AXX = mybir.AxisListType.X

B, T, C = 2, 8192, 1024
H, HS = 16, 64
NCORES = 8
SPAN = 512
NSPAN = 16
NPOS = 1024
NTOK = 2048
IC = 8
S_MAIN = NSPAN * 256
S_HI = 128
S_DIAG = 512
S_TOT = S_MAIN + S_HI + S_DIAG
NCH_MAIN = 33
NCH = 37
KAUG = 67

STAGE = 3
DEBUG_BH0 = False


class Bld:
    ENGINES = ("tensor", "vector", "scalar", "sync", "gpsimd")

    def __init__(self, nc, stack):
        self.nc = nc
        self.stack = stack
        self.ops = {e: [] for e in self.ENGINES}
        self.semh = {}
        self.semv = {}

    def _sem(self, name):
        if name not in self.semh:
            self.semh[name] = self.stack.enter_context(self.nc.semaphore(name))
            self.semv[name] = 0
        return self.semh[name]

    def emit(self, eng, fn, waits=(), inc=None):
        tgt = None
        if inc is not None:
            self._sem(inc[0])
            self.semv[inc[0]] += inc[1]
            tgt = (inc[0], inc[1], self.semv[inc[0]])
        for s, _ in waits:
            self._sem(s)
        self.ops[eng].append((tuple(waits), tgt, fn))
        return None if tgt is None else tgt[2]

    def dma(self, q, out, in_, waits=(), sem="d", slow=False):
        return self.emit(q, lambda e, o=out, i=in_, sl=slow: e.dma_start(
            out=o, in_=i, allow_slow_non_contiguous=sl), waits, (sem, 16))

    def v(self, name):
        return self.semv.get(name, 0)

    def install(self, block):
        semh = self.semh
        refs = {}
        for ename in self.ENGINES:
            for waits, tgt, fn in self.ops[ename]:
                for sv, vv in waits:
                    refs.setdefault(sv, set()).add(vv)
        remap = {}
        newtot = {}
        for ename in self.ENGINES:
            for waits, tgt, fn in self.ops[ename]:
                if tgt is not None and tgt[1] == 1:
                    sv, _, vv = tgt
                    if vv in refs.get(sv, ()):
                        newtot[sv] = newtot.get(sv, 0) + 1
                        remap.setdefault(sv, {})[vv] = newtot[sv]
        for ename in self.ENGINES:
            ops = self.ops[ename]
            if not ops:
                continue

            def body(e, ops=ops):
                seen = {}
                for waits, tgt, fn in ops:
                    for sv, vv in waits:
                        if vv <= 0:
                            continue
                        if sv in remap:
                            vv = remap[sv][vv]
                        if vv > seen.get(sv, 0):
                            e.wait_ge(semh[sv], vv)
                            seen[sv] = vv
                    ins = fn(e)
                    if tgt is not None:
                        sv, amt, vv = tgt
                        if amt == 1 and sv in remap:
                            if vv in remap[sv]:
                                if ins is None:
                                    ins = e.nop()
                                ins.then_inc(semh[sv], 1)
                        else:
                            if ins is None:
                                ins = e.nop()
                            ins.then_inc(semh[sv], amt)
            getattr(block, ename)(body)


def _emit_ln(b, sb, PS, x_sb, y_sb, scr, ready, tag):
    nc = b.nc
    sq = b.emit("scalar", lambda e: e.activation(out=scr["xsq"][:, :], in_=x_sb[:, :],
                                                func=AF.Square), ready, ("s", 1))
    mm_done = None
    for blk in range(4):
        for ic in range(IC):
            w = list(ready) if ic == 0 else ()
            b.emit("tensor", lambda e, ic=ic, blk=blk: e.matmul(
                PS[0:1, blk * 512:(blk + 1) * 512], scr["ones128f"][:, :],
                x_sb[:, ic * 2048 + blk * 512: ic * 2048 + (blk + 1) * 512],
                start=(ic == 0), stop=(ic == IC - 1)), w)
    for blk in range(4):
        for ic in range(IC):
            w = [("s", sq)] if ic == 0 else ()
            mm_done = b.emit("tensor", lambda e, ic=ic, blk=blk: e.matmul(
                PS[0:1, 2048 + blk * 512: 2048 + (blk + 1) * 512], scr["ones128b"][:, :],
                scr["xsq"][:, ic * 2048 + blk * 512: ic * 2048 + (blk + 1) * 512],
                start=(ic == 0), stop=(ic == IC - 1)),
                w, ("t", 1) if ic == IC - 1 else None)
    v1 = b.emit("vector", lambda e: e.tensor_scalar_mul(scr["mrow"][:, :], PS[0:1, 0:2048],
                1.0 / C), [("t", mm_done)], ("v", 1))
    v3 = b.emit("vector", lambda e: e.tensor_mul(scr["trow"][:, :], scr["mrow"][:, :],
                scr["mrow"][:, :]), [("v", v1)], ("v", 1))
    v4 = b.emit("vector", lambda e: e.scalar_tensor_tensor(scr["trow"][:, :],
                PS[0:1, 2048:4096], 1.0 / C, scr["trow"][:, :],
                op0=ALU.mult, op1=ALU.subtract), [("v", v3)], ("v", 1))
    s2 = b.emit("scalar", lambda e: e.activation(out=scr["rrow"][:, :], in_=scr["trow"][:, :],
                func=AF.Sqrt, bias=scr["eps"][0:1, :]), [("v", v4)], ("s", 1))
    v5 = b.emit("vector", lambda e: e.reciprocal(scr["trow"][:, :], scr["rrow"][:, :]),
                [("s", s2)], ("v", 1))
    mb = scr["mrb"]
    t1 = None
    for blk in range(4):
        t1 = b.emit("tensor", lambda e, blk=blk: e.matmul(
            PS[:, 2048 + blk * 512: 2048 + (blk + 1) * 512], scr["ones1f"][:, :],
            scr["mrow"][0:1, blk * 512:(blk + 1) * 512], start=True, stop=True),
            [("v", v4)] if blk == 0 else (), ("t", 1) if blk == 3 else None)
    ev1 = b.emit("vector", lambda e: e.tensor_copy(mb[:, 0:2048], PS[:, 2048:4096]),
                 [("t", t1)], ("v", 1))
    for blk in range(4):
        t1 = b.emit("tensor", lambda e, blk=blk: e.matmul(
            PS[:, 2048 + blk * 512: 2048 + (blk + 1) * 512], scr["ones1f"][:, :],
            scr["trow"][0:1, blk * 512:(blk + 1) * 512], start=True, stop=True),
            [("v", ev1), ("v", v5)] if blk == 0 else (), ("t", 1) if blk == 3 else None)
    ev2 = b.emit("vector", lambda e: e.tensor_copy(mb[:, 2048:4096], PS[:, 2048:4096]),
                 [("t", t1)], ("v", 1))
    outw = []
    for ic in range(IC):
        a1 = b.emit("vector", lambda e, ic=ic: e.tensor_sub(
            y_sb[:, ic * 2048:(ic + 1) * 2048], x_sb[:, ic * 2048:(ic + 1) * 2048],
            mb[:, 0:2048]), [("v", ev2)], ("v", 1))
        a2 = b.emit("vector", lambda e, ic=ic: e.tensor_mul(
            y_sb[:, ic * 2048:(ic + 1) * 2048], y_sb[:, ic * 2048:(ic + 1) * 2048],
            mb[:, 2048:4096]), [("v", a1)], ("v", 1))
        outw.append(("v", a2))
    return outw[-2:]


def _wload(b, dst, W_ap, ot):
    src = W_ap[:, ot * 128:(ot + 1) * 128].rearrange("(ic p) o -> p (ic o)", p=128)
    return b.dma("gpsimd", dst[:, :], src, (), "dg")


def _emit_ffn(b, sb, PS, y_sb, x_sb, W, cols, yready, colsready, tag, do_ucol):
    nc = b.nc
    ring = [sb[f"{tag}wr{i}"] for i in range(8)]
    ring_free = [b.v("t")] * 8
    rslot = [0]

    def load_block(wname, ot):
        i = rslot[0] % 8
        rslot[0] += 1
        w = [("t", ring_free[i])] if ring_free[i] else ()
        dv = b.emit("gpsimd", lambda e, i=i, wname=wname, ot=ot: e.dma_start(
            out=ring[i][:, :].rearrange("p (ic o) -> p ic o", ic=8),
            in_=W[wname][:, ot * 128:(ot + 1) * 128].rearrange(
                "(ic p) o -> p ic o", p=128)), w, (f"w{i}", 16))
        return i, dv

    def mark_ring_used(i, tval):
        ring_free[i] = tval

    names = ["uf", "ui", "uc"] if do_ucol else []
    bnames = ["bf", "bi", "bc"]
    first_ucol = True
    last_mm = 0
    for xi, wname in enumerate(names):
        for ot in range(IC):
            i, dv = load_block(wname, ot)
            tv = None
            for icx in range(IC):
                w = [(f"w{i}", dv)] if icx == 0 else []
                if first_ucol and icx == 0:
                    w = w + list(yready)
                    first_ucol = False
                tv = b.emit("tensor", lambda e, i=i, icx=icx, xi=xi, ot=ot: e.matmul(
                    PS[:, 3584 + xi * 8 + ot: 3584 + xi * 8 + ot + 1],
                    ring[i][:, icx * 128:(icx + 1) * 128],
                    cols["in_b"][:, icx:icx + 1],
                    start=(icx == 0), stop=(icx == IC - 1)),
                    w, ("t", 1) if icx == IC - 1 else None)
            mark_ring_used(i, tv)
            last_mm = tv
    for xi in range(len(names)):
        for ot in range(IC):
            b.emit("vector", lambda e, xi=xi, ot=ot, bn=bnames[xi]: e.tensor_add(
                sb[f"{tag}ucb"][:, xi * 8 + ot: xi * 8 + ot + 1],
                PS[:, 3584 + xi * 8 + ot: 3584 + xi * 8 + ot + 1],
                cols[bn][:, ot:ot + 1]), [("t", last_mm)] + list(colsready), ("v", 1))
    ucb_done = ("v", b.v("v"))
    if not do_ucol:
        ucb_done = ("v", 0)

    gate_names = ["wf", "wi", "wc"]
    act_f = [AF.Sigmoid, AF.Sigmoid, AF.Tanh]
    prev_acts = 0
    for ot in range(IC):
        actw = []
        for xi, wname in enumerate(gate_names):
            i, dv = load_block(wname, ot)
            tv = None
            for icx in range(IC):
                for blk in range(2):
                    w = []
                    if icx == 0 and blk == 0:
                        w = [(f"w{i}", dv), ("s", prev_acts)] + (list(yready) if ot == 0 else [])
                    tv = b.emit("tensor", lambda e, i=i, icx=icx, xi=xi, blk=blk: e.matmul(
                        PS[:, xi * 1024 + blk * 512: xi * 1024 + (blk + 1) * 512],
                        ring[i][:, icx * 128:(icx + 1) * 128],
                        y_sb[:, icx * 2048 + blk * 512: icx * 2048 + (blk + 1) * 512],
                        start=(icx == 0), stop=(icx == IC - 1)), w,
                        ("t", 1) if (icx == IC - 1 and blk == 1) else None)
            mark_ring_used(i, tv)
            av = b.emit("scalar", lambda e, xi=xi, ot=ot: e.activation(
                out=sb[f"{tag}g{xi}"][:, :], in_=PS[:, xi * 1024:(xi + 1) * 1024],
                func=act_f[xi], bias=sb[f"{tag}ucb"][:, xi * 8 + ot: xi * 8 + ot + 1]),
                [("t", tv), ucb_done], ("s", 1))
            actw.append(av)
        prev_acts = actw[-1]
        m1 = b.emit("vector", lambda e: e.tensor_mul(sb[f"{tag}t0"][:, :],
                    sb[f"{tag}g1"][:, :], sb[f"{tag}g2"][:, :]),
                    [("s", actw[1]), ("s", actw[2])], ("v", 1))
        b.emit("vector", lambda e, ot=ot: e.scalar_tensor_tensor(
            sb["c1"][:, ot * 1024:(ot + 1) * 1024], sb[f"{tag}g0"][:, :],
            cols["in_f"][:, ot:ot + 1], sb[f"{tag}t0"][:, :],
            op0=ALU.mult, op1=ALU.add), [("v", m1), ("s", actw[0])], ("v", 1))
    c1_done = ("v", b.v("v"))

    prev_acts = 0
    for ot in range(IC):
        actw = []
        for xi, (wname, uname) in enumerate([("wf", "uf"), ("wi", "ui"), ("wc", "uc")]):
            iw, dvw = load_block(wname, ot)
            iu, dvu = load_block(uname, ot)
            tv = None
            for icx in range(IC):
                for blk in range(2):
                    w = []
                    if icx == 0 and blk == 0:
                        w = [(f"w{iw}", dvw), ("s", prev_acts), c1_done]
                    tv = b.emit("tensor", lambda e, iw=iw, icx=icx, xi=xi, blk=blk: e.matmul(
                        PS[:, xi * 1024 + blk * 512: xi * 1024 + (blk + 1) * 512],
                        ring[iw][:, icx * 128:(icx + 1) * 128],
                        y_sb[:, icx * 2048 + 1024 + blk * 512: icx * 2048 + 1024 + (blk + 1) * 512],
                        start=(icx == 0), stop=False), w)
            for icx in range(IC):
                for blk in range(2):
                    w = [(f"w{iu}", dvu)] if (icx == 0 and blk == 0) else ()
                    tv = b.emit("tensor", lambda e, iu=iu, icx=icx, xi=xi, blk=blk: e.matmul(
                        PS[:, xi * 1024 + blk * 512: xi * 1024 + (blk + 1) * 512],
                        ring[iu][:, icx * 128:(icx + 1) * 128],
                        sb["c1"][:, icx * 1024 + blk * 512: icx * 1024 + (blk + 1) * 512],
                        start=False, stop=(icx == IC - 1)), w,
                        ("t", 1) if (icx == IC - 1 and blk == 1) else None)
            mark_ring_used(iw, tv)
            mark_ring_used(iu, tv)
            av = b.emit("scalar", lambda e, xi=xi, ot=ot, bn=bnames[xi]: e.activation(
                out=sb[f"{tag}g{xi}"][:, :], in_=PS[:, xi * 1024:(xi + 1) * 1024],
                func=act_f[xi], bias=cols[bn][:, ot:ot + 1]),
                [("t", tv)], ("s", 1))
            actw.append(av)
        prev_acts = actw[-1]
        m1 = b.emit("vector", lambda e: e.tensor_mul(sb[f"{tag}t0"][:, :],
                    sb[f"{tag}g1"][:, :], sb[f"{tag}g2"][:, :]),
                    [("s", actw[1]), ("s", actw[2])], ("v", 1))
        m2 = b.emit("vector", lambda e, ot=ot: e.tensor_mul(sb[f"{tag}t1"][:, :],
                    sb[f"{tag}g0"][:, :], sb["c1"][:, ot * 1024:(ot + 1) * 1024]),
                    [("s", actw[0])], ("v", 1))
        b.emit("vector", lambda e, ot=ot: e.tensor_add(
            sb["c2"][:, ot * 1024:(ot + 1) * 1024], sb[f"{tag}t1"][:, :],
            sb[f"{tag}t0"][:, :]), [("v", m1), ("v", m2)], ("v", 1))
    c2_done = ("v", b.v("v"))

    xw = []
    prev_acts = b.v("s")
    for ot in range(IC):
        iw, dvw = load_block("wo", ot)
        iu, dvu = load_block("uo", ot)
        tv = None
        for bb in range(2):
            for icx in range(IC):
                for blk in range(2):
                    w = []
                    if icx == 0 and blk == 0:
                        w = [(f"w{iw}", dvw), ("s", prev_acts)]
                    b.emit("tensor", lambda e, iw=iw, icx=icx, bb=bb, blk=blk: e.matmul(
                        PS[:, bb * 1024 + blk * 512: bb * 1024 + (blk + 1) * 512],
                        ring[iw][:, icx * 128:(icx + 1) * 128],
                        y_sb[:, icx * 2048 + bb * 1024 + blk * 512:
                             icx * 2048 + bb * 1024 + (blk + 1) * 512],
                        start=(icx == 0), stop=False), w)
            for icx in range(IC):
                for blk in range(2):
                    w = []
                    if icx == 0 and blk == 0:
                        w = [(f"w{iu}", dvu), c2_done]
                    tv = b.emit("tensor", lambda e, iu=iu, icx=icx, bb=bb, blk=blk: e.matmul(
                        PS[:, bb * 1024 + blk * 512: bb * 1024 + (blk + 1) * 512],
                        ring[iu][:, icx * 128:(icx + 1) * 128],
                        sb["c2"][:, icx * 1024 + blk * 512: icx * 1024 + (blk + 1) * 512],
                        start=False, stop=(icx == IC - 1)), w,
                        ("t", 1) if (icx == IC - 1 and blk == 1) else None)
        mark_ring_used(iw, tv)
        mark_ring_used(iu, tv)
        for bb in range(2):
            av = b.emit("scalar", lambda e, bb=bb, ot=ot: e.activation(
                out=sb[f"{tag}g{bb}"][:, :], in_=PS[:, bb * 1024:(bb + 1) * 1024],
                func=AF.Tanh, bias=cols["bo"][:, ot:ot + 1]),
                [("t", tv)], ("s", 1))
            if bb == 1:
                prev_acts = av
            uv = b.emit("vector", lambda e, ot=ot, bb=bb: e.scalar_tensor_tensor(
                x_sb[:, ot * 2048 + bb * 1024: ot * 2048 + (bb + 1) * 1024],
                sb[f"{tag}g{bb}"][:, :], 0.5,
                x_sb[:, ot * 2048 + bb * 1024: ot * 2048 + (bb + 1) * 1024],
                op0=ALU.mult, op1=ALU.add), [("s", av)], ("v", 1))
            xw.append(("v", uv))
    return xw[-2:]





def _span_slot_tables():
    lev = []
    off = 0
    for L in (4, 8, 16, 32, 64, 128, 256, 512):
        n = SPAN // L
        for i in range(n):
            lev.append((L, i))
    assert len(lev) == 255
    return lev


def _emit_qkv(b, sb, PS, d, y_sb, q_d, cin, tails_d, yready, iden):
    nc = b.nc
    ring = [sb[f"Fwr{i}"] for i in range(8)]
    ring_free = [b.v("t")] * 8
    rslot = [0]
    TR = sb["TR"]
    TR2 = sb["TR2"]
    KT = sb["KT"]
    Kst = sb["Kst"]
    Vtr = sb["Vtr"]
    qtile = sb["qtile"]
    prev_q_dma = 0

    prev_evac = 0
    tr_evac = 0
    for j in range(24):
        i = rslot[0] % 8
        rslot[0] += 1
        w = [("t", ring_free[i])] if ring_free[i] else ()
        dv = b.emit("gpsimd", lambda e, i=i, j=j: e.dma_start(
            out=ring[i][:, :].rearrange("p (ic o) -> p ic o", ic=8),
            in_=d["WattnT"][:, j * 128:(j + 1) * 128].rearrange(
                "(ic p) o -> p ic o", p=128)), w, (f"w{i}", 16))
        tv = None
        for icx in range(IC):
            for blk in range(4):
                w = []
                if icx == 0 and blk == 0:
                    w = [(f"w{i}", dv), ("v", prev_evac)] + (list(yready) if j == 0 else [])
                tv = b.emit("tensor", lambda e, i=i, icx=icx, blk=blk: e.matmul(
                    PS[:, blk * 512:(blk + 1) * 512],
                    ring[i][:, icx * 128:(icx + 1) * 128],
                    y_sb[:, icx * 2048 + blk * 512: icx * 2048 + (blk + 1) * 512],
                    start=(icx == 0), stop=(icx == IC - 1)), w,
                    ("t", 1) if (icx == IC - 1 and blk == 3) else None)
        ring_free[i] = tv
        if j < 8:
            prev_evac = b.emit("vector", lambda e, j=j: e.tensor_copy(
                qtile[:, :], PS[:, 0:2048]),
                [("t", tv), ("dq", prev_q_dma)], ("v", 1))
            prev_q_dma = b.dma("sync", q_d[j, :, :], qtile[:, :],
                               [("v", prev_evac)], "dq")
            continue
        kv = 0 if j < 16 else 1
        hp = j - 8 - kv * 8
        ev = b.emit("vector", lambda e: e.tensor_copy(sb["kt"][:, :], PS[:, 0:2048]),
                    [("t", tv)], ("v", 1))
        prev_evac = ev
        for bq in range(2):
            for sl in range(2):
                base = bq * 1024 + sl * 512
                q0 = sb["kt"][:, base:base + 512]
                g1 = b.emit("vector", lambda e, q0=q0: e.tensor_add(
                    TR[:, 0:128], q0[:, 0:512:4], q0[:, 1:512:4]), [("v", ev)], ("v", 1))
                g2 = b.emit("vector", lambda e, q0=q0: e.tensor_add(
                    TR2[:, 0:128], q0[:, 2:512:4], q0[:, 3:512:4]), [("v", g1)], ("v", 1))
                g3 = b.emit("vector", lambda e: e.tensor_add(
                    TR[:, 0:128], TR[:, 0:128], TR2[:, 0:128]), [("v", g2)], ("v", 1))
                ofs = [(0, 128, 128), (128, 64, 192), (192, 32, 224), (224, 16, 240),
                       (240, 8, 248), (248, 4, 252), (252, 2, 254)]
                gv = g3
                for (src, n2, dst) in ofs:
                    n = n2 // 2
                    gv = b.emit("vector", lambda e, src=src, n=n, dst=dst: e.tensor_add(
                        TR[:, dst:dst + n], TR[:, src:src + 2 * n:2],
                        TR[:, src + 1:src + 2 * n:2]), [("v", gv)], ("v", 1))
                if kv == 0:
                    gz = b.emit("vector", lambda e: e.memset(
                        Kst[:, 255:256], 0.0), [("dc", b.v("dc"))], ("v", 1))
                    gc = b.emit("vector", lambda e: e.tensor_copy(
                        Kst[:, 0:255], TR[:, 0:255]), [("v", gv)], ("v", 1))
                    for hh in range(2):
                        b.dma("sync", cin[sl, bq, 2 * hp + hh, 0:16384].rearrange(
                            "(a c) -> a c", a=64),
                            Kst[hh * 64:(hh + 1) * 64, :],
                            [("v", gc), ("v", gz)], "dc")
                else:
                    gv = b.emit("vector", lambda e: e.memset(TR[:, 255:256], 0.0),
                                [("v", gv)], ("v", 1))
                    for hh in range(2):
                        for ch in range(2):
                            pcol = 3584 + (2 * hh + ch) * 64
                            tt = b.emit("tensor", lambda e, hh=hh, ch=ch, pcol=pcol:
                                e.transpose(PS[:, pcol:pcol + 64],
                                            TR[hh * 64:(hh + 1) * 64,
                                               ch * 128:(ch + 1) * 128],
                                            iden[hh * 64:(hh + 1) * 64,
                                                 hh * 64:(hh + 1) * 64]),
                                [("v", gv), ("v", tr_evac)], ("t", 1))
                            tr_evac = b.emit("vector", lambda e, pcol=pcol, hh=hh,
                                             ch=ch: e.tensor_copy(
                                Vtr[:, hh * 128 + ch * 64: hh * 128 + (ch + 1) * 64],
                                PS[:, pcol:pcol + 64]),
                                [("t", tt), ("dc", b.v("dc"))], ("v", 1))
                    for hh in range(2):
                        b.dma("sync", cin[sl, bq, 2 * hp + hh, 16384:32768].rearrange(
                            "(a c) -> a c", a=128),
                            Vtr[:, hh * 128:(hh + 1) * 128],
                            [("v", tr_evac)], "dc")
                p0 = b.emit("vector", lambda e, q0=q0: e.tensor_copy(
                    KT[:, 0:512:4], q0[:, 0:512:4]), [("v", ev)], ("v", 1))
                for ph in range(1, 4):
                    p0 = b.emit("vector", lambda e, q0=q0, ph=ph: e.tensor_add(
                        KT[:, ph:512:4], KT[:, ph - 1:512:4], q0[:, ph:512:4]),
                        [("v", p0)], ("v", 1))
                for hh in range(2):
                    b.dma("gpsimd", tails_d[bq, 2 * hp + hh, kv, :,
                                            sl * 512:(sl + 1) * 512],
                          KT[hh * 64:(hh + 1) * 64, :], [("v", p0)], "dtl")
    return [("dc", b.v("dc")), ("dtl", b.v("dtl")), ("dq", b.v("dq"))]


def _emit_attn(b, sb, PS, d, q_d, cin, cga, tails_d, attn_sb,
               ag_done, iden):
    Ksb = [sb["Ksb0"], sb["Ksb1"]]
    Vsb = [sb["Vsb0"], sb["Vsb1"]]
    qa = [sb["qa0"], sb["qa1"]]
    ktl = [sb["ktl0"], sb["ktl1"]]
    vtl = [sb["vtl0"], sb["vtl1"]]
    P = [sb["P0"], sb["P1"], sb["P2"]]
    khi = sb["khi"]
    vhi16 = sb["vhi16"]
    vhs = sb["vhs"]
    vtop = sb["vtop"]
    mcols = sb["mcols"]
    mrow = sb["mrow8"]
    mt = sb["mt"]
    tt = sb["ttail"]
    trowf = sb["trowf"]
    etail = sb["etail"]
    en = sb["en"]
    erc = sb["erc"]
    dro = sb["dro"]
    atmp = sb["atmp"]
    dm = sb["dm"]
    tb = sb["tb"]
    tq = sb["tq"]
    o1f = sb["ones1f"]
    o64f = sb["ones64f"]

    b.dma("gpsimd", sb["vcol_sb"][:, :], d["vcol"][:, :], (), "dg")
    for par in range(2):
        b.dma("gpsimd", Ksb[par][65:67, :], d["killrow"][:, :], (), "dg")
        b.emit("gpsimd", lambda e, par=par: e.memset(Ksb[par][64:65, :], 1.0),
               (), ("g", 1))
        b.emit("gpsimd", lambda e, par=par: e.memset(
            Ksb[par][0:64, S_MAIN + 15:S_MAIN + S_HI], 0.0), (), ("g", 1))
        b.emit("gpsimd", lambda e, par=par: e.memset(
            Vsb[par][:, 32 * 65: 32 * 65 + 64], 0.0), (), ("g", 1))
        b.dma("gpsimd", qa[par][65:67, :], d["qkill"][:, :], (), "dg")
    b.emit("gpsimd", lambda e: e.memset(sb["Kc"][:, :], 0.0), (), ("g", 1))
    b.dma("gpsimd", dm[:, :], d["diagmask"][:, :], (), "dg")
    b.dma("sync", tb[:, :], d["topbias"][:, :], (), "d")
    b.dma("gpsimd", tq[:, :], d["topbiasqs"][:, :], (), "dg")
    init_dg = b.v("dg")
    init_d = b.v("d")
    for par in range(2):
        b.emit("vector", lambda e, par=par: e.tensor_copy(
            Vsb[par][:, :].rearrange("p (ch c) -> p ch c", c=65)[:, :, 64:65],
            sb["vcol_sb"][:, :].rearrange("p (ch c) -> p ch c", c=1)),
            [("dg", init_dg)], ("v", 1))
    initw = [("dg", init_dg), ("g", b.v("g")), ("d", init_d), ("v", b.v("v"))]

    sched = [list(range(0, 14)) + [33, 34, 32], list(range(0, 30)) + [35, 36, 32]]

    last_k = [0, 0]
    last_v = [0, 0]
    last_q = [0, 0]
    last_kt = [0, 0]
    last_kread_v = [0, 0]
    last_qread_v = [0, 0]
    last_vt = [0, 0]
    last_epi = 0
    prev_tv1 = 0
    prev_vhid = 0
    prev_mre = 0
    prev_th1 = 0
    prev_ttop = 0
    prev_se1 = 0
    prev_e4 = 0
    prev_t_end = 0
    last_qkexp = {}
    last_avP = {}
    prev_adma = 0
    adma = 0

    for bh in range(32):
        bq, h = bh // 16, bh % 16
        par = bh % 2
        hp, hh = h // 2, h % 2
        kw = [("t", last_k[par]), ("v", last_kread_v[par])] + list(ag_done) + initw
        for sg in range(NSPAN):
            r = sg if sg <= 7 else 15 - sg
            sl = 0 if sg <= 7 else 1
            b.dma("sync", Ksb[par][0:64, sg * 256:(sg + 1) * 256],
                  cga[r, sl, bq, h, 0:16384].rearrange("(a c) -> a c", a=64),
                  kw if sg == 0 else (), f"dK{par}")
        b.dma("gpsimd", Ksb[par][0:64, S_MAIN + S_HI:S_TOT].rearrange(
            "p (sl c) -> p sl c", sl=2),
            cin[:, bq, h, 0:16384].rearrange("sl (p c) -> p sl c", p=64),
            [("t", last_k[par]), ("v", last_kread_v[par])], f"gd{par}")
        kfills = [(f"dK{par}", b.v(f"dK{par}")), (f"gd{par}", b.v(f"gd{par}"))]
        kv1 = b.emit("vector", lambda e, par=par: e.tensor_add(
            khi[:, 0:8], Ksb[par][0:64, 254:S_MAIN:512],
            Ksb[par][0:64, 510:S_MAIN:512]), kfills, ("v", 1))
        for (src, n) in ((0, 4), (8, 2), (12, 1)):
            kv1 = b.emit("vector", lambda e, src=src, n=n: e.tensor_add(
                khi[:, 8 + src // 2: 8 + src // 2 + n],
                khi[:, src:src + 2 * n:2], khi[:, src + 1:src + 2 * n:2]),
                [("v", kv1)], ("v", 1))
        khid = b.emit("vector", lambda e, par=par: e.tensor_copy(
            Ksb[par][0:64, S_MAIN:S_MAIN + 15], khi[:, 0:15]),
            [("v", kv1), ("t", last_k[par])], ("v", 1))
        for sg in range(NSPAN):
            r = sg if sg <= 7 else 15 - sg
            sl = 0 if sg <= 7 else 1
            b.dma("sync", Vsb[par][:, :].rearrange(
                "p (ch c) -> p ch c", c=65)[:, 2 * sg:2 * sg + 2, 0:64],
                cga[r, sl, bq, h, 16384:32768].rearrange(
                    "(a ch c) -> a ch c", a=128, ch=2),
                [("t", last_v[par])] if sg == 0 else (), f"dV{par}")
        for sl in range(2):
            b.dma("gpsimd", Vsb[par][:, :].rearrange(
                "p (ch c) -> p ch c", c=65)[:, 33 + 2 * sl:35 + 2 * sl, 0:64],
                cin[sl, bq, h, 16384:32768].rearrange("(p ch c) -> p ch c",
                                                      p=128, ch=2),
                [("t", last_v[par])] if sl == 0 else (), f"gd{par}")
        b.dma("gpsimd", vhi16[0:8, :],
              cga[:, 0, bq, h, 16384 + 16192:16384 + 16192 + 64],
              list(ag_done) + [("t", prev_th1)], "gv")
        vg2 = b.dma("gpsimd", vhi16[8:16, :],
                    cga[:, 1, bq, h, 16384 + 16192:16384 + 16192 + 64], (), "gv")
        th1 = b.emit("tensor", lambda e: e.transpose(
            PS[0:64, 3584:3600], vhi16[:, :], sb["permat"][0:16, 0:16]),
            [("gv", vg2), ("v", prev_mre)], ("t", 1))
        prev_th1 = th1
        vv = b.emit("vector", lambda e: e.tensor_copy(vhs[:, 0:16], PS[0:64, 3584:3600]),
                    [("t", th1)], ("v", 1))
        vv = b.emit("vector", lambda e: e.tensor_add(
            vtop[:, 0:8], vhs[:, 0:16:2], vhs[:, 1:16:2]), [("v", vv)], ("v", 1))
        for (src, n) in ((0, 4), (8, 2), (12, 1)):
            vv = b.emit("vector", lambda e, src=src, n=n: e.tensor_add(
                vtop[:, 8 + src // 2: 8 + src // 2 + n],
                vtop[:, src:src + 2 * n:2], vtop[:, src + 1:src + 2 * n:2]),
                [("v", vv)], ("v", 1))
        th2 = b.emit("tensor", lambda e: e.transpose(
            PS[0:16, 3600:3664], vtop[:, :], iden[0:64, 0:64]),
            [("v", vv)], ("t", 1))
        vhid = b.emit("vector", lambda e, par=par: e.tensor_copy(
            Vsb[par][0:15, 32 * 65: 32 * 65 + 64], PS[0:15, 3600:3664]),
            [("t", th2), ("t", last_v[par])], ("v", 1))
        prev_vhid = vhid
        ktv = b.dma("sync", ktl[par][:, :], tails_d[bq, h, 0, :, :],
                    [("v", last_kt[par])], f"dt{par}")
        vtv = b.dma("sync", vtl[par][:, :], tails_d[bq, h, 1, :, :],
                    [("v", last_vt[par])], f"dt{par}")
        qv = b.dma("gpsimd", qa[par][0:64, :],
                   q_d[hp, hh * 64:(hh + 1) * 64,
                       bq * 1024:(bq + 1) * 1024],
                   [("t", last_q[par]), ("v", last_qread_v[par])], f"gq{par}")
        kcv = 0
        Kc = sb["Kc"]
        for (lo, hi, dst) in ((240, 248, 0), (248, 252, 128), (252, 254, 192),
                              (254, 255, 224)):
            n = hi - lo
            kcv = b.emit("vector", lambda e, par=par, lo=lo, n=n, dst=dst:
                e.tensor_copy(
                    Kc[0:64, dst:dst + 16 * n].rearrange("p (sg i) -> p sg i", sg=16),
                    Ksb[par][0:64, 0:S_MAIN].rearrange(
                        "p (sg i) -> p sg i", sg=16)[:, :, lo:lo + n]),
                (kfills + [("t", prev_ttop)]) if lo == 240 else [("v", kcv)],
                ("v", 1))
        kcv = b.emit("vector", lambda e, par=par: e.tensor_copy(
            Kc[0:64, 240:255], Ksb[par][0:64, S_MAIN:S_MAIN + 15]),
            [("v", kcv), ("v", khid)], ("v", 1))
        last_kread_v[par] = kcv
        mslot_free = [0, 0]
        tr_prev = 0
        mv = 0
        for qt in range(8):
            r_ = qt % 2
            w = [("v", mslot_free[r_])]
            if qt < 2:
                w.append(("s", last_qkexp.get(r_, 0)))
            if qt == 0:
                w += [(f"gq{par}", qv), ("v", kcv), ("v", prev_tv1)]
            t_top = b.emit("tensor", lambda e, par=par, qt=qt, r_=r_: e.matmul(
                PS[:, r_ * 512:r_ * 512 + 256],
                qa[par][0:64, qt * 128:(qt + 1) * 128],
                Kc[0:64, :], start=True, stop=True), w, ("t", 1))
            prev_ttop = t_top
            tm = b.emit("vector", lambda e, par=par, qt=qt: e.tensor_mul(
                tt[:, :], qa[par][0:64, qt * 128:(qt + 1) * 128],
                ktl[par][0:64, qt * 128:(qt + 1) * 128]),
                [(f"dt{par}", vtv), ("t", tr_prev)] if qt > 0
                else [(f"dt{par}", vtv), (f"gq{par}", qv)], ("v", 1))
            last_kt[par] = tm
            tc = b.emit("tensor", lambda e, qt=qt: e.matmul(
                PS[:, 3328 + qt: 3328 + qt + 1], tt[:, :], o64f[:, :],
                start=True, stop=True), [("v", tm)], ("t", 1))
            tr_prev = b.emit("tensor", lambda e, qt=qt: e.matmul(
                PS[0:1, 2048 + qt * 128: 2048 + (qt + 1) * 128], o64f[:, :],
                tt[:, :], start=True, stop=True),
                [("v", last_epi)] if qt == 0 else (), ("t", 1))
            m1 = b.emit("vector", lambda e, qt=qt, r_=r_: e.tensor_add(
                mt[:, :], PS[:, r_ * 512:r_ * 512 + 256],
                tq[:, qt * 256:(qt + 1) * 256]), [("t", t_top)], ("v", 1))
            m2 = b.emit("vector", lambda e, qt=qt: e.reduce_max(
                mcols[:, qt:qt + 1], mt[:, :], axis=AXX), [("v", m1)], ("v", 1))
            mslot_free[r_] = m2
            m3 = b.emit("vector", lambda e, qt=qt: e.tensor_max(
                mcols[:, qt:qt + 1], mcols[:, qt:qt + 1],
                PS[:, 3328 + qt:3328 + qt + 1]), [("v", m2), ("t", tc)], ("v", 1))
            mv = b.emit("vector", lambda e, qt=qt: e.tensor_scalar(
                mcols[:, qt:qt + 1], mcols[:, qt:qt + 1], 0.08, -1.0,
                op0=ALU.max, op1=ALU.mult), [("v", m3)], ("v", 1))
        tmt = b.emit("tensor", lambda e: e.transpose(
            PS[0:8, 3712:3840], mcols[:, :], iden[:, :]), [("v", mv)], ("t", 1))
        mre = b.emit("vector", lambda e: e.tensor_copy(mrow[:, :], PS[0:8, 3712:3840]),
                     [("t", tmt)], ("v", 1))
        prev_mre = mre
        mrd = b.dma("gpsimd", qa[par][64:65, :].rearrange("p (a c) -> p a c", a=8),
                    mrow[:, :].rearrange("p (o c) -> p o c", o=1), [("v", mre)],
                    f"gq{par}")
        tv1 = b.emit("vector", lambda e, par=par: e.tensor_add(
            trowf[:, :], PS[0:1, 2048:3072], qa[par][64:65, :]),
            [(f"gq{par}", mrd), ("s", prev_se1)], ("v", 1))
        prev_tv1 = tv1
        last_qread_v[par] = tv1
        se1 = b.emit("scalar", lambda e: e.activation(
            out=etail[:, :], in_=trowf[:, :], func=AF.Exp, scale=0.125),
            [("v", tv1), ("t", prev_t_end)], ("s", 1))
        prev_se1 = se1
        for blk in range(2):
            b.emit("tensor", lambda e, blk=blk: e.matmul(
                PS[0:64, 2048 + blk * 512: 2048 + (blk + 1) * 512],
                o1f[0:1, 0:64], etail[0:1, blk * 512:(blk + 1) * 512],
                start=True, stop=True), [("s", se1)] if blk == 0 else ())
        kready = [(f"dK{par}", b.v(f"dK{par}")), (f"dV{par}", b.v(f"dV{par}")),
                  (f"gd{par}", b.v(f"gd{par}")), (f"gq{par}", b.v(f"gq{par}")),
                  ("v", khid), ("v", vhid)]
        epi_h0 = 0
        pbc = 0
        for half in range(2):
            avbank = 1024 + half * 512
            chunks = sched[half]
            for ci, ch in enumerate(chunks):
                r = pbc % 2
                qw = list(kready) if (ci == 0 and half == 0) else []
                if last_qkexp.get(r):
                    qw.append(("s", last_qkexp[r]))
                qk = b.emit("tensor", lambda e, par=par, ch=ch, r=r, half=half:
                    e.matmul(PS[:, r * 512:(r + 1) * 512],
                             Ksb[par][:, ch * 128:(ch + 1) * 128],
                             qa[par][:, half * 512:(half + 1) * 512],
                             start=True, stop=True), qw, ("t", 1))
                last_k[par] = qk
                last_q[par] = qk
                expw = [("t", qk)]
                if ch == 32:
                    bb = b.emit("vector", lambda e, r=r, half=half: e.tensor_add(
                        PS[0:15, r * 512:(r + 1) * 512],
                        PS[0:15, r * 512:(r + 1) * 512],
                        tb[0:15, half * 512:(half + 1) * 512]),
                        [("t", qk)], ("v", 1))
                    expw.append(("v", bb))
                if ch in (33, 34, 35, 36):
                    dmc = 0 if ch in (33, 35) else 1
                    bb = b.emit("vector", lambda e, r=r, dmc=dmc: e.tensor_add(
                        PS[:, r * 512:(r + 1) * 512],
                        PS[:, r * 512:(r + 1) * 512],
                        dm[:, dmc * 512:(dmc + 1) * 512]),
                        [("t", qk), ("dg", init_dg)], ("v", 1))
                    expw.append(("v", bb))
                pb = pbc % 3
                pbc += 1
                if last_avP.get(pb):
                    expw.append(("t", last_avP[pb]))
                ex = b.emit("scalar", lambda e, r=r, pb=pb: e.activation(
                    out=P[pb][:, :], in_=PS[:, r * 512:(r + 1) * 512],
                    func=AF.Exp, scale=0.125), expw, ("s", 1))
                last_qkexp[r] = ex
                avw = [("s", ex)]
                if ci == 0:
                    avw.append(("v", epi_h0 if half == 1 else last_epi))
                av = b.emit("tensor", lambda e, par=par, ch=ch, pb=pb, avbank=avbank,
                            st=(ci == 0), sp=(ci == len(chunks) - 1):
                    e.matmul(PS[0:65, avbank:avbank + 512],
                             Vsb[par][:, ch * 65:(ch + 1) * 65], P[pb][:, :],
                             start=st, stop=sp), avw, ("t", 1))
                last_avP[pb] = av
                last_v[par] = av
            e1 = b.emit("vector", lambda e, par=par, half=half: e.tensor_mul(
                en[:, :], PS[0:64, 2048 + half * 512: 2048 + (half + 1) * 512],
                vtl[par][0:64, half * 512:(half + 1) * 512]),
                [("t", b.v("t")), (f"dt{par}", vtv)], ("v", 1))
            last_vt[par] = e1
            e2 = b.emit("vector", lambda e, avbank=avbank: e.tensor_add(
                en[:, :], en[:, :], PS[0:64, avbank:avbank + 512]),
                [("v", e1)], ("v", 1))
            e3 = b.emit("vector", lambda e, half=half, avbank=avbank: e.tensor_add(
                dro[:, :], PS[64:65, avbank:avbank + 512],
                etail[0:1, half * 512:(half + 1) * 512]),
                [("v", e2), ("t", prev_e4)], ("v", 1))
            denb = 1536 - half * 512
            e4 = b.emit("tensor", lambda e, denb=denb: e.matmul(
                PS[0:64, denb:denb + 512], o1f[0:1, 0:64], dro[0:1, :],
                start=True, stop=True), [("v", e3)], ("t", 1))
            prev_e4 = e4
            if DEBUG_BH0 and bh == 0 and half == 0:
                dst = sb["dbgsta"]
                vv9 = b.emit("vector", lambda e, avbank=avbank: e.tensor_copy(
                    dst[:, 0:512], PS[0:65, avbank:avbank + 512]),
                    [("v", e3)], ("v", 1))
                b.dma("sync", d["dbg_av"][:, :], dst[0:65, 0:512], [("v", vv9)], "do")
                b.dma("sync", d["dbg_dro"][:, :], dro[:, :], [("v", vv9)], "do")
                b.dma("sync", d["dbg_etail"][:, :], etail[:, :], [("v", vv9)], "do")
                b.dma("sync", d["dbg_mcols"][:, :], mcols[:, :], [("v", vv9)], "do")
                b.dma("sync", d["dbg_qa"][:, :], qa[par][:, :], [("v", vv9)], "do")
                b.dma("sync", d["dbg_ksb"][:, :], Ksb[par][:, :], [("v", vv9)], "do")
                b.dma("sync", d["dbg_vsb"][:, :], Vsb[par][:, :], [("v", vv9)], "do")
                b.dma("sync", d["dbg_en"][:, :], en[:, :], [("v", vv9)], "do")
                fin = b.dma("sync", d["dbg_trow"][:, :], trowf[:, :],
                            [("v", vv9)], "do")
                b.emit("sync", lambda e: e.nop(), [("do", fin)])
                return [("do", fin)]
            e5 = b.emit("vector", lambda e, denb=denb: e.reciprocal(
                erc[:, :], PS[0:64, denb:denb + 512]), [("t", e4)], ("v", 1))
            e6 = b.emit("vector", lambda e: e.tensor_mul(atmp[:, :], en[:, :],
                        erc[:, :]), [("v", e5), ("ga", prev_adma)], ("v", 1))
            if half == 0:
                epi_h0 = e6
            adma = b.dma("gpsimd", attn_sb[hh * 64:(hh + 1) * 64,
                                           hp * 2048 + bq * 1024 + half * 512:
                                           hp * 2048 + bq * 1024 + (half + 1) * 512],
                         atmp[:, :], [("v", e6)], "ga")
            prev_adma = adma
        last_epi = e6
        prev_t_end = b.v("t")
    return [("ga", adma), ("v", b.v("v"))]


def _emit_proj(b, sb, PS, d, attn_sb, x_sb, attn_done):
    ring = [sb[f"Fwr{i}"] for i in range(8)]
    ring_free = [b.v("t")] * 8
    prev = 0
    xw = []
    for ot in range(IC):
        i = ot % 8
        w = [("t", ring_free[i])] if ring_free[i] else ()
        dv = b.emit("gpsimd", lambda e, i=i, ot=ot: e.dma_start(
            out=ring[i][:, :].rearrange("p (ic o) -> p ic o", ic=8),
            in_=d["WprojT"][:, ot * 128:(ot + 1) * 128].rearrange(
                "(ic p) o -> p ic o", p=128)), w, (f"w{i}", 16))
        tv = None
        for icx in range(IC):
            for blk in range(4):
                w = []
                if icx == 0 and blk == 0:
                    w = [(f"w{i}", dv), ("v", prev)] + (list(attn_done) if ot == 0 else [])
                tv = b.emit("tensor", lambda e, i=i, icx=icx, blk=blk: e.matmul(
                    PS[:, blk * 512:(blk + 1) * 512],
                    ring[i][:, icx * 128:(icx + 1) * 128],
                    attn_sb[:, icx * 2048 + blk * 512: icx * 2048 + (blk + 1) * 512],
                    start=(icx == 0), stop=(icx == IC - 1)), w,
                    ("t", 1) if (icx == IC - 1 and blk == 3) else None)
        ring_free[i] = tv
        prev = b.emit("vector", lambda e, ot=ot: e.tensor_add(
            x_sb[:, ot * 2048:(ot + 1) * 2048], x_sb[:, ot * 2048:(ot + 1) * 2048],
            PS[:, 0:2048]), [("t", tv)], ("v", 1))
        xw.append(("v", prev))
    return xw[-2:]


def build_nc(stage):
    nc = bass.Bass("TRN2", target_bir_lowering=False, debug=False)
    d = {}
    d["xT"] = nc.dram_tensor("xT", [C, NTOK], F32, kind="ExternalInput").ap()
    for w in ("wfT", "ufT", "wiT", "uiT", "wcT", "ucT", "woT", "uoT", "WprojT"):
        d[w] = nc.dram_tensor(w, [C, C], F32, kind="ExternalInput").ap()
    d["WattnT"] = nc.dram_tensor("WattnT", [C, 3 * C], F32, kind="ExternalInput").ap()
    for v in ("bf", "bi", "bc", "bo", "in_c"):
        d[v] = nc.dram_tensor(v, [C, 1], F32, kind="ExternalInput").ap()
    d["xout"] = nc.dram_tensor("xout", [C, NTOK], F32, kind="ExternalOutput").ap()
    d["iden"] = nc.dram_tensor("iden", [128, 128], F32, kind="ExternalInput").ap()
    cin_t = cga_t = None
    if stage >= 2:
        d["diagmask"] = nc.dram_tensor("diagmask", [128, 1024], F32,
                                       kind="ExternalInput").ap()
        d["killrow"] = nc.dram_tensor("killrow", [2, S_TOT], F32,
                                      kind="ExternalInput").ap()
        d["topbias"] = nc.dram_tensor("topbias", [15, 1024], F32,
                                      kind="ExternalInput").ap()
        d["topbiasqs"] = nc.dram_tensor("topbiasqs", [128, 2048], F32,
                                        kind="ExternalInput").ap()
        d["vcol"] = nc.dram_tensor("vcol", [128, NCH], F32,
                                   kind="ExternalInput").ap()
        d["qkill"] = nc.dram_tensor("qkill", [2, 1024], F32,
                                    kind="ExternalInput").ap()
        d["permat"] = nc.dram_tensor("permat", [16, 16], F32,
                                     kind="ExternalInput").ap()
        if DEBUG_BH0:
            for nm, shp, dt_ in (("dbg_av", [65, 512], F32), ("dbg_dro", [1, 512], F32),
                                 ("dbg_etail", [1, 1024], F32),
                                 ("dbg_mcols", [128, 8], F32),
                                 ("dbg_qa", [KAUG, 1024], BF16),
                                 ("dbg_ksb", [KAUG, S_TOT], BF16),
                                 ("dbg_vsb", [128, NCH * 65], BF16),
                                 ("dbg_en", [64, 512], F32),
                                 ("dbg_trow", [1, 1024], F32)):
                d[nm] = nc.dram_tensor(nm, shp, dt_, kind="ExternalOutput").ap()
        cin_t = nc.dram_tensor("cin", [2, 2, 16, 32768], BF16)
        cga_t = nc.dram_tensor("cga", [8, 2, 2, 16, 32768], BF16)
        d["cin"] = cin_t.ap()
        d["cga"] = cga_t.ap()
        d["tails"] = nc.dram_tensor("tails", [2, 16, 2, 64, 1024], BF16).ap()
        d["q_d"] = nc.dram_tensor("q_d", [8, 128, NTOK], BF16).ap()

    stack = ExitStack()
    b = Bld(nc, stack)
    with stack:
        PS = stack.enter_context(nc.psum_tensor("PS", [128, 4096], F32))
        sb = {}

        def alloc(name, shape, dt, st=None):
            sb[name] = (st or stack).enter_context(nc.sbuf_tensor(name, shape, dt))
            return sb[name]

        x_sb = alloc("x", [128, IC * NTOK], F32)
        y_sb = alloc("y", [128, IC * NTOK], BF16)
        scr = {}
        scr["ones128f"] = alloc("ones128f", [128, 1], F32)
        scr["ones128b"] = alloc("ones128b", [128, 1], BF16)
        scr["ones1f"] = alloc("ones1f", [1, 128], F32)
        scr["eps"] = alloc("epsc", [128, 1], F32)
        iden_sb = alloc("iden_sb", [128, 128], F32)
        sb["ones64f"] = alloc("ones64f", [64, 1], F32)
        for tagn in ("F",):
            for i in range(8):
                alloc(f"{tagn}wr{i}", [128, 1024], BF16)
            for i in range(3):
                alloc(f"{tagn}g{i}", [128, 1024], BF16)
            alloc(f"{tagn}t0", [128, 1024], BF16)
            alloc(f"{tagn}t1", [128, 1024], BF16)
            alloc(f"{tagn}ucb", [128, 24], F32)
        cols = {}
        for v, cn in (("in_c", "in_f"), ("bf", "bf"), ("bi", "bi"), ("bc", "bc"),
                      ("bo", "bo")):
            cols[cn] = alloc("col_" + cn, [128, IC], F32)
        cols["in_b"] = alloc("col_in_b", [128, IC], BF16)

        def _ln_scope(ph, tagn):
            scr["xsq"] = alloc("xsq" + tagn, [128, IC * NTOK], BF16, ph)
            lnr = alloc("lnrows" + tagn, [1, 3 * NTOK], F32, ph)
            scr["mrow"] = lnr[0:1, 0:NTOK]
            scr["trow"] = lnr[0:1, NTOK:2 * NTOK]
            scr["rrow"] = lnr[0:1, 2 * NTOK:3 * NTOK]
            scr["mrb"] = alloc("mrb" + tagn, [128, 2 * NTOK], F32, ph)

        b.emit("gpsimd", lambda e: e.memset(scr["ones128f"][:, :], 1.0), (), ("g", 1))
        b.emit("gpsimd", lambda e: e.memset(scr["ones128b"][:, :], 1.0), (), ("g", 1))
        b.emit("gpsimd", lambda e: e.memset(scr["ones1f"][:, :], 1.0), (), ("g", 1))
        b.emit("gpsimd", lambda e: e.memset(scr["eps"][:, :], 1e-5), (), ("g", 1))
        b.emit("gpsimd", lambda e: e.memset(sb["ones64f"][:, :], 1.0), (), ("g", 1))
        b.dma("sync", iden_sb[:, :], d["iden"][:, :])
        ginit = ("g", b.v("g"))
        xw = []
        for ic in range(IC):
            dv = b.dma("sync", x_sb[:, ic * NTOK:(ic + 1) * NTOK],
                       d["xT"][ic * 128:(ic + 1) * 128, :])
            xw.append(("d", dv))
        for v, cn in (("in_c", "in_f"), ("bf", "bf"), ("bi", "bi"), ("bc", "bc"),
                      ("bo", "bo")):
            src = d[v].rearrange("(ic p) o -> p ic o", p=128)
            b.dma("sync", cols[cn][:, :].rearrange("p (ic o) -> p ic o", o=1),
                  src, slow=True)
        b.dma("gpsimd", cols["in_b"][:, :].rearrange("p (ic o) -> p ic o", o=1),
              d["in_c"].rearrange("(ic p) o -> p ic o", p=128), (), "dg", slow=True)
        loads = [("d", b.v("d")), ("dg", b.v("dg")), ginit]

        W = {"wf": d["wfT"], "uf": d["ufT"], "wi": d["wiT"], "ui": d["uiT"],
             "wc": d["wcT"], "uc": d["ucT"], "wo": d["woT"], "uo": d["uoT"]}

        if stage == 0:
            xdone = loads
        elif stage == -1:
            with ExitStack() as ph:
                _ln_scope(ph, "1")
                yready = _emit_ln(b, sb, PS, x_sb, y_sb, scr, loads, "L1")
            xdone = []
            for ic in range(IC):
                vv = b.emit("vector", lambda e, ic=ic: e.tensor_copy(
                    x_sb[:, ic * NTOK:(ic + 1) * NTOK],
                    y_sb[:, ic * NTOK:(ic + 1) * NTOK]), yready, ("v", 1))
                xdone.append(("v", vv))
            xdone = xdone[-2:]
        if stage >= 1:
            with ExitStack() as ph:
                _ln_scope(ph, "1")
                yready = _emit_ln(b, sb, PS, x_sb, y_sb, scr, loads, "L1")
            with ExitStack() as ph:
                sb["c1"] = alloc("c1", [128, IC * 1024], BF16, ph)
                sb["c2"] = alloc("c2", [128, IC * 1024], BF16, ph)
                xdone = _emit_ffn(b, sb, PS, y_sb, x_sb, W, cols, yready, loads, "F", True)

        if stage >= 2:
            with ExitStack() as ph:
                _ln_scope(ph, "2")
                yready = _emit_ln(b, sb, PS, x_sb, y_sb, scr, xdone, "L2")
            with ExitStack() as ph:
                sb["TR"] = alloc("TR", [128, 256], F32, ph)
                sb["TR2"] = alloc("TR2", [128, 128], F32, ph)
                sb["KT"] = alloc("KT", [128, 512], F32, ph)
                sb["Kst"] = alloc("Kst", [128, 256], BF16, ph)
                sb["Vtr"] = alloc("Vtr", [128, 256], BF16, ph)
                sb["kt"] = alloc("kt", [128, NTOK], BF16, ph)
                sb["qtile"] = alloc("qtile", [128, NTOK], BF16, ph)
                contrib = _emit_qkv(b, sb, PS, d, y_sb, d["q_d"], d["cin"],
                                    d["tails"], yready, iden_sb)
                agv = b.emit("gpsimd", lambda e: e.collective_compute(
                    "AllGather", ALU.bypass,
                    replica_groups=[list(range(NCORES))],
                    ins=[cin_t.ap()], outs=[cga_t.ap()]), contrib, ("cc", 1))
                ag_done = [("cc", agv)]
                for nm, shp, dt in (
                        ("Ksb0", [KAUG, S_TOT], BF16), ("Ksb1", [KAUG, S_TOT], BF16),
                        ("Vsb0", [128, NCH * 65], BF16), ("Vsb1", [128, NCH * 65], BF16),
                        ("qa0", [KAUG, 1024], BF16), ("qa1", [KAUG, 1024], BF16),
                        ("ktl0", [64, 1024], BF16), ("ktl1", [64, 1024], BF16),
                        ("vtl0", [64, 1024], BF16), ("vtl1", [64, 1024], BF16),
                        ("P0", [128, 512], BF16), ("P1", [128, 512], BF16),
                        ("P2", [128, 512], BF16),
                        ("khi", [64, 16], F32), ("vhi16", [16, 64], F32),
                        ("vhs", [64, 16], F32), ("vtop", [64, 16], F32),
                        ("mcols", [128, 8], F32), ("mrow8", [8, 128], BF16),
                        ("mt", [128, 256], F32), ("ttail", [64, 128], F32),
                        ("trowf", [1, 1024], F32), ("etail", [1, 1024], F32),
                        ("en", [64, 512], F32), ("erc", [64, 512], F32),
                        ("dro", [1, 512], F32), ("atmp", [64, 512], BF16),
                        ("dm", [128, 1024], BF16), ("tb", [15, 1024], F32),
                        ("tq", [128, 2048], BF16), ("permat_sb", [16, 16], F32),
                        ("Kc", [64, 256], BF16),
                        ("vcol_sb", [128, NCH], BF16)) + (
                        (("dbgsta", [65, 512], F32),) if DEBUG_BH0 else ()):
                    alloc(nm, shp, dt, ph)
                sb["permat"] = sb["permat_sb"]
                b.dma("sync", sb["permat"][:, :], d["permat"][:, :])
                attn_done = _emit_attn(b, sb, PS, d, d["q_d"], d["cin"], d["cga"],
                                       d["tails"], y_sb, ag_done, iden_sb)
            xdone = _emit_proj(b, sb, PS, d, y_sb, x_sb, attn_done)

        if stage >= 3:
            with ExitStack() as ph:
                _ln_scope(ph, "3")
                yready = _emit_ln(b, sb, PS, x_sb, y_sb, scr, xdone, "L3")
            with ExitStack() as ph:
                sb["c1"] = alloc("c1b", [128, IC * 1024], BF16, ph)
                sb["c2"] = alloc("c2b", [128, IC * 1024], BF16, ph)
                xdone = _emit_ffn(b, sb, PS, y_sb, x_sb, W, cols, yready, loads, "F", False)

        fin = []
        for ic in range(IC):
            dv = b.dma("sync", d["xout"][ic * 128:(ic + 1) * 128, :],
                       x_sb[:, ic * NTOK:(ic + 1) * NTOK], xdone, "do")
            fin.append(dv)
        b.emit("sync", lambda e: e.nop(), [("do", fin[-1])])

        with nc.Block() as block:
            b.install(block)
    return nc


def _slot_tables():
    out = []
    for L in (4, 8, 16, 32, 64, 128, 256, 512):
        for i in range(SPAN // L):
            out.append((L, i, (i + 1) * L - 1))
    return out


_R_TOP = ([1024 * (i + 1) - 1 for i in range(8)] +
          [2048 * (i + 1) - 1 for i in range(4)] +
          [4096 * (i + 1) - 1 for i in range(2)] + [8192 * 1 - 1])


def _host_constants(c):
    slots = _slot_tables()
    own = [c, NSPAN - 1 - c]
    dmv = np.full((128, 1024), -1e9, np.float32)
    for ch in range(2):
        for p in range(128):
            sl = ch * 128 + p
            if sl >= 255:
                continue
            r = slots[sl][2]
            dmv[p, ch * 512 + r: (ch + 1) * 512] = 0.0
    kill = np.zeros((2, S_TOT), np.float32)
    for half in range(2):
        m = own[half]
        for sg in range(NSPAN):
            if sg >= m:
                kill[half, sg * 256:(sg + 1) * 256] = 1.0
        kill[half, S_MAIN + S_HI + (1 - half) * 256:
             S_MAIN + S_HI + (2 - half) * 256] = 1.0
    tbv = np.zeros((15, 1024), np.float32)
    for half in range(2):
        base_t = own[half] * SPAN
        for k in range(15):
            for t in range(SPAN):
                if _R_TOP[k] > base_t + t:
                    tbv[k, half * 512 + t] = -1e9
    r_coarse = np.empty(256, np.int64)
    for col in range(128):
        sg, i = col // 8, col % 8
        r_coarse[col] = sg * SPAN + (i + 1) * 64 - 1
    for col in range(128, 192):
        sg, i = (col - 128) // 4, (col - 128) % 4
        r_coarse[col] = sg * SPAN + (i + 1) * 128 - 1
    for col in range(192, 224):
        sg, i = (col - 192) // 2, (col - 192) % 2
        r_coarse[col] = sg * SPAN + (i + 1) * 256 - 1
    for col in range(224, 240):
        r_coarse[col] = (col - 224) * SPAN + 511
    r_coarse[240:255] = _R_TOP
    r_coarse[255] = 10 ** 9
    tqv = np.zeros((128, 2048), np.float32)
    for qt in range(8):
        half = qt // 4
        for p in range(128):
            t = own[half] * SPAN + (qt % 4) * 128 + p
            tqv[p, qt * 256:(qt + 1) * 256] = np.where(r_coarse > t, -1e9, 0.0)
    vcol = np.ones((128, NCH), np.float32)
    for ch in range(32):
        if ch % 2 == 1:
            vcol[127, ch] = 0.0
    vcol[15:128, 32] = 0.0
    vcol[127, 34] = 0.0
    vcol[127, 36] = 0.0
    qk_ = np.zeros((2, 1024), np.float32)
    qk_[0, 0:512] = -8e9
    qk_[1, 512:1024] = -8e9
    pm = np.zeros((16, 16), np.float32)
    pm[0:8, 0:8] = np.eye(8)
    pm[8:16, 8:16] = np.eye(8)[::-1]
    return {"diagmask": dmv, "killrow": kill, "topbias": tbv, "topbiasqs": tqv,
            "vcol": vcol, "qkill": qk_, "permat": pm}


def _shard_positions(c):
    s0, s1 = c, NSPAN - 1 - c
    return np.concatenate([np.arange(s0 * SPAN, (s0 + 1) * SPAN),
                           np.arange(s1 * SPAN, (s1 + 1) * SPAN)])


def kernel(L0_attn_x, L0_attn_W_attn, L0_attn_W_proj, L0_ffn_wf, L0_ffn_uf, L0_ffn_bf,
           L0_ffn_wi, L0_ffn_ui, L0_ffn_bi, L0_ffn_wc, L0_ffn_uc, L0_ffn_bc,
           L0_ffn_wo, L0_ffn_uo, L0_ffn_bo, L0_ffn_in_c):
    x = np.asarray(L0_attn_x, dtype=np.float32)
    WT = {
        "WattnT": np.ascontiguousarray(np.asarray(L0_attn_W_attn, np.float32).T),
        "WprojT": np.ascontiguousarray(np.asarray(L0_attn_W_proj, np.float32).T),
        "wfT": np.ascontiguousarray(np.asarray(L0_ffn_wf, np.float32).T),
        "ufT": np.ascontiguousarray(np.asarray(L0_ffn_uf, np.float32).T),
        "wiT": np.ascontiguousarray(np.asarray(L0_ffn_wi, np.float32).T),
        "uiT": np.ascontiguousarray(np.asarray(L0_ffn_ui, np.float32).T),
        "wcT": np.ascontiguousarray(np.asarray(L0_ffn_wc, np.float32).T),
        "ucT": np.ascontiguousarray(np.asarray(L0_ffn_uc, np.float32).T),
        "woT": np.ascontiguousarray(np.asarray(L0_ffn_wo, np.float32).T),
        "uoT": np.ascontiguousarray(np.asarray(L0_ffn_uo, np.float32).T),
    }
    vecs = {
        "bf": np.asarray(L0_ffn_bf, np.float32).reshape(C, 1),
        "bi": np.asarray(L0_ffn_bi, np.float32).reshape(C, 1),
        "bc": np.asarray(L0_ffn_bc, np.float32).reshape(C, 1),
        "bo": np.asarray(L0_ffn_bo, np.float32).reshape(C, 1),
        "in_c": np.asarray(L0_ffn_in_c, np.float32).reshape(C, 1),
    }
    nc = build_nc(STAGE)
    iden = np.eye(128, dtype=np.float32)
    in_maps = []
    for c in range(NCORES):
        pos = _shard_positions(c)
        xs = x[:, pos, :]
        xT = np.ascontiguousarray(xs.transpose(2, 0, 1).reshape(C, NTOK))
        m = {"xT": xT, "iden": iden}
        m.update(WT)
        m.update(vecs)
        if STAGE >= 2:
            m.update(_host_constants(c))
        in_maps.append(m)
    res = run_bass_kernel_spmd(nc, in_maps, core_ids=list(range(NCORES)))
    out = np.empty((B, T, C), np.float32)
    for c in range(NCORES):
        pos = _shard_positions(c)
        y = res.results[c]["xout"]
        out[:, pos, :] = y.T.reshape(B, NPOS, C)
    return out


# version 05
# speedup vs baseline: 1.0456x; 1.0456x over previous
import numpy as np
from contextlib import ExitStack
import concourse.bass as bass
import concourse.mybir as mybir
from concourse.bass_utils import run_bass_kernel_spmd

F32 = mybir.dt.float32
BF16 = mybir.dt.bfloat16
AF = mybir.ActivationFunctionType
ALU = mybir.AluOpType
AXX = mybir.AxisListType.X

B, T, C = 2, 8192, 1024
H, HS = 16, 64
NCORES = 8
SPAN = 512
NSPAN = 16
NPOS = 1024
NTOK = 2048
IC = 8
S_MAIN = NSPAN * 256
S_HI = 128
S_DIAG = 512
S_TOT = S_MAIN + S_HI + S_DIAG
NCH_MAIN = 33
NCH = 37
KAUG = 67

STAGE = 3
DEBUG_BH0 = False
QKV_PARTS = 63


class Bld:
    ENGINES = ("tensor", "vector", "scalar", "sync", "gpsimd")

    def __init__(self, nc, stack):
        self.nc = nc
        self.stack = stack
        self.ops = {e: [] for e in self.ENGINES}
        self.semh = {}
        self.semv = {}

    def _sem(self, name):
        if name not in self.semh:
            self.semh[name] = self.stack.enter_context(self.nc.semaphore(name))
            self.semv[name] = 0
        return self.semh[name]

    def emit(self, eng, fn, waits=(), inc=None):
        tgt = None
        if inc is not None:
            self._sem(inc[0])
            self.semv[inc[0]] += inc[1]
            tgt = (inc[0], inc[1], self.semv[inc[0]])
        for s, _ in waits:
            self._sem(s)
        self.ops[eng].append((tuple(waits), tgt, fn))
        return None if tgt is None else tgt[2]

    def dma(self, q, out, in_, waits=(), sem="d", slow=False):
        return self.emit(q, lambda e, o=out, i=in_, sl=slow: e.dma_start(
            out=o, in_=i, allow_slow_non_contiguous=sl), waits, (sem, 16))

    def v(self, name):
        return self.semv.get(name, 0)

    def install(self, block):
        semh = self.semh
        refs = {}
        for ename in self.ENGINES:
            for waits, tgt, fn in self.ops[ename]:
                for sv, vv in waits:
                    refs.setdefault(sv, set()).add(vv)
        remap = {}
        newtot = {}
        for ename in self.ENGINES:
            for waits, tgt, fn in self.ops[ename]:
                if tgt is not None and tgt[1] == 1:
                    sv, _, vv = tgt
                    if vv in refs.get(sv, ()):
                        newtot[sv] = newtot.get(sv, 0) + 1
                        remap.setdefault(sv, {})[vv] = newtot[sv]
        for ename in self.ENGINES:
            ops = self.ops[ename]
            if not ops:
                continue

            def body(e, ops=ops):
                seen = {}
                for waits, tgt, fn in ops:
                    for sv, vv in waits:
                        if vv <= 0:
                            continue
                        if sv in remap:
                            vv = remap[sv][vv]
                        if vv > seen.get(sv, 0):
                            e.wait_ge(semh[sv], vv)
                            seen[sv] = vv
                    ins = fn(e)
                    if tgt is not None:
                        sv, amt, vv = tgt
                        if amt == 1 and sv in remap:
                            if vv in remap[sv]:
                                if ins is None:
                                    ins = e.nop()
                                ins.then_inc(semh[sv], 1)
                        else:
                            if ins is None:
                                ins = e.nop()
                            ins.then_inc(semh[sv], amt)
            getattr(block, ename)(body)


def _emit_ln(b, sb, PS, x_sb, y_sb, scr, ready, tag):
    nc = b.nc
    sq = b.emit("scalar", lambda e: e.activation(out=scr["xsq"][:, :], in_=x_sb[:, :],
                                                func=AF.Square), ready, ("s", 1))
    mm_done = None
    for blk in range(4):
        for ic in range(IC):
            w = list(ready) if ic == 0 else ()
            b.emit("tensor", lambda e, ic=ic, blk=blk: e.matmul(
                PS[0:1, blk * 512:(blk + 1) * 512], scr["ones128f"][:, :],
                x_sb[:, ic * 2048 + blk * 512: ic * 2048 + (blk + 1) * 512],
                start=(ic == 0), stop=(ic == IC - 1)), w)
    for blk in range(4):
        for ic in range(IC):
            w = [("s", sq)] if ic == 0 else ()
            mm_done = b.emit("tensor", lambda e, ic=ic, blk=blk: e.matmul(
                PS[0:1, 2048 + blk * 512: 2048 + (blk + 1) * 512], scr["ones128b"][:, :],
                scr["xsq"][:, ic * 2048 + blk * 512: ic * 2048 + (blk + 1) * 512],
                start=(ic == 0), stop=(ic == IC - 1)),
                w, ("t", 1) if ic == IC - 1 else None)
    v1 = b.emit("vector", lambda e: e.tensor_scalar_mul(scr["mrow"][:, :], PS[0:1, 0:2048],
                1.0 / C), [("t", mm_done)], ("v", 1))
    v3 = b.emit("vector", lambda e: e.tensor_mul(scr["trow"][:, :], scr["mrow"][:, :],
                scr["mrow"][:, :]), [("v", v1)], ("v", 1))
    v4 = b.emit("vector", lambda e: e.scalar_tensor_tensor(scr["trow"][:, :],
                PS[0:1, 2048:4096], 1.0 / C, scr["trow"][:, :],
                op0=ALU.mult, op1=ALU.subtract), [("v", v3)], ("v", 1))
    s2 = b.emit("scalar", lambda e: e.activation(out=scr["rrow"][:, :], in_=scr["trow"][:, :],
                func=AF.Sqrt, bias=scr["eps"][0:1, :]), [("v", v4)], ("s", 1))
    v5 = b.emit("vector", lambda e: e.reciprocal(scr["trow"][:, :], scr["rrow"][:, :]),
                [("s", s2)], ("v", 1))
    mb = scr["mrb"]
    t1 = None
    for blk in range(4):
        t1 = b.emit("tensor", lambda e, blk=blk: e.matmul(
            PS[:, 2048 + blk * 512: 2048 + (blk + 1) * 512], scr["ones1f"][:, :],
            scr["mrow"][0:1, blk * 512:(blk + 1) * 512], start=True, stop=True),
            [("v", v4)] if blk == 0 else (), ("t", 1) if blk == 3 else None)
    ev1 = b.emit("vector", lambda e: e.tensor_copy(mb[:, 0:2048], PS[:, 2048:4096]),
                 [("t", t1)], ("v", 1))
    for blk in range(4):
        t1 = b.emit("tensor", lambda e, blk=blk: e.matmul(
            PS[:, 2048 + blk * 512: 2048 + (blk + 1) * 512], scr["ones1f"][:, :],
            scr["trow"][0:1, blk * 512:(blk + 1) * 512], start=True, stop=True),
            [("v", ev1), ("v", v5)] if blk == 0 else (), ("t", 1) if blk == 3 else None)
    ev2 = b.emit("vector", lambda e: e.tensor_copy(mb[:, 2048:4096], PS[:, 2048:4096]),
                 [("t", t1)], ("v", 1))
    outw = []
    for ic in range(IC):
        a1 = b.emit("vector", lambda e, ic=ic: e.tensor_sub(
            y_sb[:, ic * 2048:(ic + 1) * 2048], x_sb[:, ic * 2048:(ic + 1) * 2048],
            mb[:, 0:2048]), [("v", ev2)], ("v", 1))
        a2 = b.emit("vector", lambda e, ic=ic: e.tensor_mul(
            y_sb[:, ic * 2048:(ic + 1) * 2048], y_sb[:, ic * 2048:(ic + 1) * 2048],
            mb[:, 2048:4096]), [("v", a1)], ("v", 1))
        outw.append(("v", a2))
    return outw[-2:]


def _wload(b, dst, W_ap, ot):
    src = W_ap[:, ot * 128:(ot + 1) * 128].rearrange("(ic p) o -> p (ic o)", p=128)
    return b.dma("gpsimd", dst[:, :], src, (), "dg")


def _emit_ffn(b, sb, PS, y_sb, x_sb, W, cols, yready, colsready, tag, do_ucol):
    nc = b.nc
    ring = [sb[f"{tag}wr{i}"] for i in range(8)]
    ring_free = [b.v("t")] * 8
    rslot = [0]

    def load_block(wname, ot):
        i = rslot[0] % 8
        rslot[0] += 1
        w = [("t", ring_free[i])] if ring_free[i] else ()
        dv = b.emit("gpsimd", lambda e, i=i, wname=wname, ot=ot: e.dma_start(
            out=ring[i][:, :].rearrange("p (ic o) -> p ic o", ic=8),
            in_=W[wname][:, ot * 128:(ot + 1) * 128].rearrange(
                "(ic p) o -> p ic o", p=128)), w, (f"w{i}", 16))
        return i, dv

    def mark_ring_used(i, tval):
        ring_free[i] = tval

    names = ["uf", "ui", "uc"] if do_ucol else []
    bnames = ["bf", "bi", "bc"]
    first_ucol = True
    last_mm = 0
    for xi, wname in enumerate(names):
        for ot in range(IC):
            i, dv = load_block(wname, ot)
            tv = None
            for icx in range(IC):
                w = [(f"w{i}", dv)] if icx == 0 else []
                if first_ucol and icx == 0:
                    w = w + list(yready)
                    first_ucol = False
                tv = b.emit("tensor", lambda e, i=i, icx=icx, xi=xi, ot=ot: e.matmul(
                    PS[:, 3584 + xi * 8 + ot: 3584 + xi * 8 + ot + 1],
                    ring[i][:, icx * 128:(icx + 1) * 128],
                    cols["in_b"][:, icx:icx + 1],
                    start=(icx == 0), stop=(icx == IC - 1)),
                    w, ("t", 1) if icx == IC - 1 else None)
            mark_ring_used(i, tv)
            last_mm = tv
    for xi in range(len(names)):
        for ot in range(IC):
            b.emit("vector", lambda e, xi=xi, ot=ot, bn=bnames[xi]: e.tensor_add(
                sb[f"{tag}ucb"][:, xi * 8 + ot: xi * 8 + ot + 1],
                PS[:, 3584 + xi * 8 + ot: 3584 + xi * 8 + ot + 1],
                cols[bn][:, ot:ot + 1]), [("t", last_mm)] + list(colsready), ("v", 1))
    ucb_done = ("v", b.v("v"))
    if not do_ucol:
        ucb_done = ("v", 0)

    gate_names = ["wf", "wi", "wc"]
    act_f = [AF.Sigmoid, AF.Sigmoid, AF.Tanh]
    prev_acts = 0
    for ot in range(IC):
        actw = []
        for xi, wname in enumerate(gate_names):
            i, dv = load_block(wname, ot)
            tv = None
            for icx in range(IC):
                for blk in range(2):
                    w = []
                    if icx == 0 and blk == 0:
                        w = [(f"w{i}", dv), ("s", prev_acts)] + (list(yready) if ot == 0 else [])
                    tv = b.emit("tensor", lambda e, i=i, icx=icx, xi=xi, blk=blk: e.matmul(
                        PS[:, xi * 1024 + blk * 512: xi * 1024 + (blk + 1) * 512],
                        ring[i][:, icx * 128:(icx + 1) * 128],
                        y_sb[:, icx * 2048 + blk * 512: icx * 2048 + (blk + 1) * 512],
                        start=(icx == 0), stop=(icx == IC - 1)), w,
                        ("t", 1) if (icx == IC - 1 and blk == 1) else None)
            mark_ring_used(i, tv)
            av = b.emit("scalar", lambda e, xi=xi, ot=ot: e.activation(
                out=sb[f"{tag}g{xi}"][:, :], in_=PS[:, xi * 1024:(xi + 1) * 1024],
                func=act_f[xi], bias=sb[f"{tag}ucb"][:, xi * 8 + ot: xi * 8 + ot + 1]),
                [("t", tv), ucb_done], ("s", 1))
            actw.append(av)
        prev_acts = actw[-1]
        m1 = b.emit("vector", lambda e: e.tensor_mul(sb[f"{tag}t0"][:, :],
                    sb[f"{tag}g1"][:, :], sb[f"{tag}g2"][:, :]),
                    [("s", actw[1]), ("s", actw[2])], ("v", 1))
        b.emit("vector", lambda e, ot=ot: e.scalar_tensor_tensor(
            sb["c1"][:, ot * 1024:(ot + 1) * 1024], sb[f"{tag}g0"][:, :],
            cols["in_f"][:, ot:ot + 1], sb[f"{tag}t0"][:, :],
            op0=ALU.mult, op1=ALU.add), [("v", m1), ("s", actw[0])], ("v", 1))
    c1_done = ("v", b.v("v"))

    prev_acts = 0
    for ot in range(IC):
        actw = []
        for xi, (wname, uname) in enumerate([("wf", "uf"), ("wi", "ui"), ("wc", "uc")]):
            iw, dvw = load_block(wname, ot)
            iu, dvu = load_block(uname, ot)
            tv = None
            for icx in range(IC):
                for blk in range(2):
                    w = []
                    if icx == 0 and blk == 0:
                        w = [(f"w{iw}", dvw), ("s", prev_acts), c1_done]
                    tv = b.emit("tensor", lambda e, iw=iw, icx=icx, xi=xi, blk=blk: e.matmul(
                        PS[:, xi * 1024 + blk * 512: xi * 1024 + (blk + 1) * 512],
                        ring[iw][:, icx * 128:(icx + 1) * 128],
                        y_sb[:, icx * 2048 + 1024 + blk * 512: icx * 2048 + 1024 + (blk + 1) * 512],
                        start=(icx == 0), stop=False), w)
            for icx in range(IC):
                for blk in range(2):
                    w = [(f"w{iu}", dvu)] if (icx == 0 and blk == 0) else ()
                    tv = b.emit("tensor", lambda e, iu=iu, icx=icx, xi=xi, blk=blk: e.matmul(
                        PS[:, xi * 1024 + blk * 512: xi * 1024 + (blk + 1) * 512],
                        ring[iu][:, icx * 128:(icx + 1) * 128],
                        sb["c1"][:, icx * 1024 + blk * 512: icx * 1024 + (blk + 1) * 512],
                        start=False, stop=(icx == IC - 1)), w,
                        ("t", 1) if (icx == IC - 1 and blk == 1) else None)
            mark_ring_used(iw, tv)
            mark_ring_used(iu, tv)
            av = b.emit("scalar", lambda e, xi=xi, ot=ot, bn=bnames[xi]: e.activation(
                out=sb[f"{tag}g{xi}"][:, :], in_=PS[:, xi * 1024:(xi + 1) * 1024],
                func=act_f[xi], bias=cols[bn][:, ot:ot + 1]),
                [("t", tv)], ("s", 1))
            actw.append(av)
        prev_acts = actw[-1]
        m1 = b.emit("vector", lambda e: e.tensor_mul(sb[f"{tag}t0"][:, :],
                    sb[f"{tag}g1"][:, :], sb[f"{tag}g2"][:, :]),
                    [("s", actw[1]), ("s", actw[2])], ("v", 1))
        m2 = b.emit("vector", lambda e, ot=ot: e.tensor_mul(sb[f"{tag}t1"][:, :],
                    sb[f"{tag}g0"][:, :], sb["c1"][:, ot * 1024:(ot + 1) * 1024]),
                    [("s", actw[0])], ("v", 1))
        b.emit("vector", lambda e, ot=ot: e.tensor_add(
            sb["c2"][:, ot * 1024:(ot + 1) * 1024], sb[f"{tag}t1"][:, :],
            sb[f"{tag}t0"][:, :]), [("v", m1), ("v", m2)], ("v", 1))
    c2_done = ("v", b.v("v"))

    xw = []
    prev_acts = b.v("s")
    for ot in range(IC):
        iw, dvw = load_block("wo", ot)
        iu, dvu = load_block("uo", ot)
        tv = None
        for bb in range(2):
            for icx in range(IC):
                for blk in range(2):
                    w = []
                    if icx == 0 and blk == 0:
                        w = [(f"w{iw}", dvw), ("s", prev_acts)]
                    b.emit("tensor", lambda e, iw=iw, icx=icx, bb=bb, blk=blk: e.matmul(
                        PS[:, bb * 1024 + blk * 512: bb * 1024 + (blk + 1) * 512],
                        ring[iw][:, icx * 128:(icx + 1) * 128],
                        y_sb[:, icx * 2048 + bb * 1024 + blk * 512:
                             icx * 2048 + bb * 1024 + (blk + 1) * 512],
                        start=(icx == 0), stop=False), w)
            for icx in range(IC):
                for blk in range(2):
                    w = []
                    if icx == 0 and blk == 0:
                        w = [(f"w{iu}", dvu), c2_done]
                    tv = b.emit("tensor", lambda e, iu=iu, icx=icx, bb=bb, blk=blk: e.matmul(
                        PS[:, bb * 1024 + blk * 512: bb * 1024 + (blk + 1) * 512],
                        ring[iu][:, icx * 128:(icx + 1) * 128],
                        sb["c2"][:, icx * 1024 + blk * 512: icx * 1024 + (blk + 1) * 512],
                        start=False, stop=(icx == IC - 1)), w,
                        ("t", 1) if (icx == IC - 1 and blk == 1) else None)
        mark_ring_used(iw, tv)
        mark_ring_used(iu, tv)
        for bb in range(2):
            av = b.emit("scalar", lambda e, bb=bb, ot=ot: e.activation(
                out=sb[f"{tag}g{bb}"][:, :], in_=PS[:, bb * 1024:(bb + 1) * 1024],
                func=AF.Tanh, bias=cols["bo"][:, ot:ot + 1]),
                [("t", tv)], ("s", 1))
            if bb == 1:
                prev_acts = av
            uv = b.emit("vector", lambda e, ot=ot, bb=bb: e.scalar_tensor_tensor(
                x_sb[:, ot * 2048 + bb * 1024: ot * 2048 + (bb + 1) * 1024],
                sb[f"{tag}g{bb}"][:, :], 0.5,
                x_sb[:, ot * 2048 + bb * 1024: ot * 2048 + (bb + 1) * 1024],
                op0=ALU.mult, op1=ALU.add), [("s", av)], ("v", 1))
            xw.append(("v", uv))
    return xw[-2:]





def _span_slot_tables():
    lev = []
    off = 0
    for L in (4, 8, 16, 32, 64, 128, 256, 512):
        n = SPAN // L
        for i in range(n):
            lev.append((L, i))
    assert len(lev) == 255
    return lev


def _emit_qkv(b, sb, PS, d, y_sb, q_d, cin, tails_d, yready, iden):
    nc = b.nc
    ring = [sb[f"Fwr{i}"] for i in range(8)]
    ring_free = [b.v("t")] * 8
    rslot = [0]
    qtile = sb["qtile"]
    prev_q_dma = 0

    prev_evac = 0
    tr_evac = 0
    for j in range(24):
        i = rslot[0] % 8
        rslot[0] += 1
        w = [("t", ring_free[i])] if ring_free[i] else ()
        dv = b.emit("gpsimd", lambda e, i=i, j=j: e.dma_start(
            out=ring[i][:, :].rearrange("p (ic o) -> p ic o", ic=8),
            in_=d["WattnT"][:, j * 128:(j + 1) * 128].rearrange(
                "(ic p) o -> p ic o", p=128)), w, (f"w{i}", 16))
        tv = None
        for icx in range(IC):
            for blk in range(4):
                w = []
                if icx == 0 and blk == 0:
                    w = [(f"w{i}", dv), ("v", prev_evac)] + (list(yready) if j == 0 else [])
                tv = b.emit("tensor", lambda e, i=i, icx=icx, blk=blk: e.matmul(
                    PS[:, blk * 512:(blk + 1) * 512],
                    ring[i][:, icx * 128:(icx + 1) * 128],
                    y_sb[:, icx * 2048 + blk * 512: icx * 2048 + (blk + 1) * 512],
                    start=(icx == 0), stop=(icx == IC - 1)), w,
                    ("t", 1) if (icx == IC - 1 and blk == 3) else None)
        ring_free[i] = tv
        if j < 8:
            prev_evac = b.emit("vector", lambda e, j=j: e.tensor_copy(
                qtile[:, :], PS[:, 0:2048]),
                [("t", tv), ("dq", prev_q_dma)], ("v", 1))
            prev_q_dma = b.dma("sync", q_d[j, :, :], qtile[:, :],
                               [("v", prev_evac)], "dq")
            continue
        kv = 0 if j < 16 else 1
        hp = j - 8 - kv * 8
        ev = b.emit("vector", lambda e: e.tensor_copy(sb["kt"][:, :], PS[:, 0:2048]),
                    [("t", tv)], ("v", 1))
        prev_evac = ev
        if not (QKV_PARTS & 1):
            continue
        TR4 = sb["TR4"]
        kt4 = sb["kt"][:, :].rearrange("p (q c) -> p q c", q=4)
        t4 = TR4[:, :].rearrange("p (q c) -> p q c", q=4)
        gv = b.emit("vector", lambda e: e.tensor_add(
            t4[:, :, 0:128], kt4[:, :, 0:512:4], kt4[:, :, 1:512:4]),
            [("v", ev)], ("v", 1))
        gv = b.emit("vector", lambda e: e.tensor_add(
            t4[:, :, 0:128], t4[:, :, 0:128], kt4[:, :, 2:512:4]),
            [("v", gv)], ("v", 1))
        gv = b.emit("vector", lambda e: e.tensor_add(
            t4[:, :, 0:128], t4[:, :, 0:128], kt4[:, :, 3:512:4]),
            [("v", gv)], ("v", 1))
        for (src, n2, dst) in ((0, 128, 128), (128, 64, 192), (192, 32, 224),
                               (224, 16, 240), (240, 8, 248), (248, 4, 252),
                               (252, 2, 254)):
            n = n2 // 2
            gv = b.emit("vector", lambda e, src=src, n=n, dst=dst: e.tensor_add(
                t4[:, :, dst:dst + n], t4[:, :, src:src + 2 * n:2],
                t4[:, :, src + 1:src + 2 * n:2]), [("v", gv)], ("v", 1))
        KT4 = sb["KT4"][:, :]
        do_tails = bool(QKV_PARTS & 2)
        if do_tails:
            p0 = b.emit("vector", lambda e: e.tensor_copy(
                KT4[:, 0:2048:4], sb["kt"][:, 0:2048:4]), [("v", ev)], ("v", 1))
            for ph in range(1, 4):
                p0 = b.emit("vector", lambda e, ph=ph: e.tensor_add(
                    KT4[:, ph:2048:4], KT4[:, ph - 1:2048:4], sb["kt"][:, ph:2048:4]),
                    [("v", p0)], ("v", 1))
            for bq in range(2):
                for hh in range(2):
                    b.dma("gpsimd", tails_d[bq, 2 * hp + hh, kv, :, :],
                          KT4[hh * 64:(hh + 1) * 64, :].rearrange(
                              "p (bq c) -> p bq c", bq=2)[:, bq, :],
                          [("v", p0)], "dtl")
        if kv == 0 and not (QKV_PARTS & 4):
            continue
        if kv == 1 and not (QKV_PARTS & 8):
            continue
        if kv == 0:
            gz = b.emit("vector", lambda e: e.memset(
                sb["Kst4"][:, :].rearrange("p (q c) -> p q c", q=4)[:, :, 255:256],
                0.0), [("dc", b.v("dc"))], ("v", 1))
            gc = b.emit("vector", lambda e: e.tensor_copy(
                sb["Kst4"][:, :].rearrange("p (q c) -> p q c", q=4)[:, :, 0:255],
                t4[:, :, 0:255]), [("v", gv)], ("v", 1))
            for bq in range(2):
                for sl in range(2):
                    for hh in range(2):
                        b.dma("sync", cin[sl, bq, 2 * hp + hh, 0:16384].rearrange(
                            "(a c) -> a c", a=64),
                            sb["Kst4"][hh * 64:(hh + 1) * 64,
                                       (bq * 2 + sl) * 256:(bq * 2 + sl + 1) * 256],
                            [("v", gc), ("v", gz)], "dc")
        else:
            gv = b.emit("vector", lambda e: e.memset(
                t4[:, :, 255:256], 0.0), [("v", gv)], ("v", 1))
            for grp in range(4):
                tt = None
                for hh in range(2):
                    for ch in range(2):
                        pcol = 3584 + (2 * hh + ch) * 64
                        w = [("v", gv), ("v", tr_evac)]
                        if hh == 1 and ch == 0:
                            w.append(("t", tt))
                        tt = b.emit("tensor", lambda e, grp=grp, hh=hh, ch=ch,
                                    pcol=pcol: e.transpose(
                            PS[:, pcol:pcol + 64],
                            TR4[hh * 64:(hh + 1) * 64,
                                grp * 256 + ch * 128: grp * 256 + (ch + 1) * 128],
                            iden[hh * 64:(hh + 1) * 64, hh * 64:(hh + 1) * 64]),
                            w, ("t", 1))
                tr_evac = b.emit("vector", lambda e, grp=grp: e.tensor_copy(
                    sb["Vtr4"][:, grp * 256:(grp + 1) * 256],
                    PS[:, 3584:3840]), [("t", tt), ("dc", b.v("dc"))], ("v", 1))
            if QKV_PARTS & 32:
                for bq in range(2):
                    for sl in range(2):
                        grp = bq * 2 + sl
                        for hh in range(2):
                            b.dma("sync",
                                  cin[sl, bq, 2 * hp + hh, 16384:32768].rearrange(
                                      "(a c) -> a c", a=128),
                                  sb["Vtr4"][:, grp * 256 + hh * 128:
                                             grp * 256 + (hh + 1) * 128],
                                  [("v", tr_evac)], "dc")
    return [("dc", b.v("dc")), ("dtl", b.v("dtl")), ("dq", b.v("dq"))]


def _emit_attn(b, sb, PS, d, q_d, cin, cga, tails_d, attn_sb,
               ag_done, iden):
    Ksb = [sb["Ksb0"], sb["Ksb1"]]
    Vsb = [sb["Vsb0"], sb["Vsb1"]]
    qa = [sb["qa0"], sb["qa1"]]
    ktl = [sb["ktl0"], sb["ktl1"]]
    vtl = [sb["vtl0"], sb["vtl1"]]
    P = [sb["P0"], sb["P1"], sb["P2"]]
    khi = sb["khi"]
    vhi16 = sb["vhi16"]
    vhs = sb["vhs"]
    vtop = sb["vtop"]
    mcols = sb["mcols"]
    mrow = sb["mrow8"]
    mt = sb["mt"]
    tt = sb["ttail"]
    trowf = sb["trowf"]
    etail = sb["etail"]
    en = sb["en"]
    erc = sb["erc"]
    dro = sb["dro"]
    atmp = sb["atmp"]
    dm = sb["dm"]
    tb = sb["tb"]
    tq = sb["tq"]
    o1f = sb["ones1f"]
    o1b = sb["ones1b"]
    o64f = sb["ones64f"]

    b.dma("gpsimd", sb["vcol_sb"][:, :], d["vcol"][:, :], (), "dg")
    for par in range(2):
        b.dma("gpsimd", Ksb[par][65:67, :], d["killrow"][:, :], (), "dg")
        b.emit("gpsimd", lambda e, par=par: e.memset(Ksb[par][64:65, :], 1.0),
               (), ("g", 1))
        b.emit("gpsimd", lambda e, par=par: e.memset(
            Ksb[par][0:64, S_MAIN + 15:S_MAIN + S_HI], 0.0), (), ("g", 1))
        b.emit("gpsimd", lambda e, par=par: e.memset(
            Vsb[par][:, 32 * 65: 32 * 65 + 64], 0.0), (), ("g", 1))
        b.dma("gpsimd", qa[par][65:67, :], d["qkill"][:, :], (), "dg")
    b.emit("gpsimd", lambda e: e.memset(sb["Kc"][:, :], 0.0), (), ("g", 1))
    b.dma("gpsimd", dm[:, :], d["diagmask"][:, :], (), "dg")
    b.dma("gpsimd", tb[:, :], d["topbias"][:, :], (), "dg")
    b.dma("gpsimd", tq[:, :], d["topbiasqs"][:, :], (), "dg")
    init_dg = b.v("dg")
    init_d = b.v("d")
    for par in range(2):
        b.emit("vector", lambda e, par=par: e.tensor_copy(
            Vsb[par][:, :].rearrange("p (ch c) -> p ch c", c=65)[:, :, 64:65],
            sb["vcol_sb"][:, :].rearrange("p (ch c) -> p ch c", c=1)),
            [("dg", init_dg)], ("v", 1))
    initw = [("dg", init_dg), ("g", b.v("g")), ("d", init_d), ("v", b.v("v"))]

    sched = [list(range(0, 14)) + [33, 34, 32], list(range(0, 30)) + [35, 36, 32]]

    last_k = [0, 0]
    last_v = [0, 0]
    last_q = [0, 0]
    last_kt = [0, 0]
    last_kread_v = [0, 0]
    last_qread_v = [0, 0]
    last_vt = [0, 0]
    last_epi = 0
    prev_tv1 = 0
    prev_vhid = 0
    prev_mre = 0
    prev_th1 = 0
    prev_ttop = 0
    prev_se1 = 0
    prev_e4 = 0
    prev_t_end = 0
    last_qkexp = {}
    last_avP = {}
    prev_adma = 0
    adma = 0

    for bh in range(32):
        bq, h = bh // 16, bh % 16
        par = bh % 2
        hp, hh = h // 2, h % 2
        kw = [("t", last_k[par]), ("v", last_kread_v[par])] + list(ag_done) + initw
        b.dma("sync", Ksb[par][0:64, 0:8 * 256].rearrange("p (r c) -> p r c", r=8),
              cga[:, 0, bq, h, 0:16384].rearrange("r (a c) -> a r c", a=64),
              kw, f"dK{par}")
        for r in range(8):
            b.dma("sync", Ksb[par][0:64, (15 - r) * 256:(16 - r) * 256],
                  cga[r, 1, bq, h, 0:16384].rearrange("(a c) -> a c", a=64),
                  (), f"dK{par}")
        b.dma("gpsimd", Ksb[par][0:64, S_MAIN + S_HI:S_TOT].rearrange(
            "p (sl c) -> p sl c", sl=2),
            cin[:, bq, h, 0:16384].rearrange("sl (p c) -> p sl c", p=64),
            [("t", last_k[par]), ("v", last_kread_v[par])], f"gd{par}")
        kfills = [(f"dK{par}", b.v(f"dK{par}")), (f"gd{par}", b.v(f"gd{par}"))]
        kv1 = b.emit("vector", lambda e, par=par: e.tensor_add(
            khi[:, 0:8], Ksb[par][0:64, 254:S_MAIN:512],
            Ksb[par][0:64, 510:S_MAIN:512]), kfills, ("v", 1))
        for (src, n) in ((0, 4), (8, 2), (12, 1)):
            kv1 = b.emit("vector", lambda e, src=src, n=n: e.tensor_add(
                khi[:, 8 + src // 2: 8 + src // 2 + n],
                khi[:, src:src + 2 * n:2], khi[:, src + 1:src + 2 * n:2]),
                [("v", kv1)], ("v", 1))
        khid = b.emit("vector", lambda e, par=par: e.tensor_copy(
            Ksb[par][0:64, S_MAIN:S_MAIN + 15], khi[:, 0:15]),
            [("v", kv1), ("t", last_k[par])], ("v", 1))
        vmain = Vsb[par][:, 0:32 * 65].rearrange("p (r ch c) -> p r ch c",
                                                 r=16, c=65)
        for ch_ in range(2):
            b.dma("sync", vmain[:, 0:8, ch_, 0:64],
                  cga[:, 0, bq, h, 16384:32768].rearrange(
                      "r (a ch c) -> a r ch c", a=128, ch=2)[:, :, ch_, :],
                  [("t", last_v[par])] if ch_ == 0 else (), f"dV{par}")
            for r in range(8):
                b.dma("sync", vmain[:, 15 - r, ch_, 0:64],
                      cga[r, 1, bq, h, 16384:32768].rearrange(
                          "(a ch c) -> a ch c", a=128, ch=2)[:, ch_, :],
                      (), f"dV{par}")
        for sl in range(2):
            b.dma("gpsimd", Vsb[par][:, :].rearrange(
                "p (ch c) -> p ch c", c=65)[:, 33 + 2 * sl:35 + 2 * sl, 0:64],
                cin[sl, bq, h, 16384:32768].rearrange("(p ch c) -> p ch c",
                                                      p=128, ch=2),
                [("t", last_v[par])] if sl == 0 else (), f"gd{par}")
        b.dma("gpsimd", vhi16[0:8, :],
              cga[:, 0, bq, h, 16384 + 16192:16384 + 16192 + 64],
              list(ag_done) + [("t", prev_th1)], "gv")
        vg2 = b.dma("gpsimd", vhi16[8:16, :],
                    cga[:, 1, bq, h, 16384 + 16192:16384 + 16192 + 64], (), "gv")
        th1 = b.emit("tensor", lambda e: e.transpose(
            PS[0:64, 3584:3600], vhi16[:, :], sb["permat"][0:16, 0:16]),
            [("gv", vg2), ("v", prev_mre)], ("t", 1))
        prev_th1 = th1
        vv = b.emit("vector", lambda e: e.tensor_copy(vhs[:, 0:16], PS[0:64, 3584:3600]),
                    [("t", th1)], ("v", 1))
        vv = b.emit("vector", lambda e: e.tensor_add(
            vtop[:, 0:8], vhs[:, 0:16:2], vhs[:, 1:16:2]), [("v", vv)], ("v", 1))
        for (src, n) in ((0, 4), (8, 2), (12, 1)):
            vv = b.emit("vector", lambda e, src=src, n=n: e.tensor_add(
                vtop[:, 8 + src // 2: 8 + src // 2 + n],
                vtop[:, src:src + 2 * n:2], vtop[:, src + 1:src + 2 * n:2]),
                [("v", vv)], ("v", 1))
        th2 = b.emit("tensor", lambda e: e.transpose(
            PS[0:16, 3600:3664], vtop[:, :], iden[0:64, 0:64]),
            [("v", vv)], ("t", 1))
        vhid = b.emit("vector", lambda e, par=par: e.tensor_copy(
            Vsb[par][0:15, 32 * 65: 32 * 65 + 64], PS[0:15, 3600:3664]),
            [("t", th2), ("t", last_v[par])], ("v", 1))
        prev_vhid = vhid
        ktv = b.dma("sync", ktl[par][:, :], tails_d[bq, h, 0, :, :],
                    [("v", last_kt[par])], f"dt{par}")
        vtv = b.dma("sync", vtl[par][:, :], tails_d[bq, h, 1, :, :],
                    [("v", last_vt[par])], f"dt{par}")
        qv = b.dma("gpsimd", qa[par][0:64, :],
                   q_d[hp, hh * 64:(hh + 1) * 64,
                       bq * 1024:(bq + 1) * 1024],
                   [("t", last_q[par]), ("v", last_qread_v[par])], f"gq{par}")
        kcv = 0
        Kc = sb["Kc"]
        for (lo, hi, dst) in ((240, 248, 0), (248, 252, 128), (252, 254, 192),
                              (254, 255, 224)):
            n = hi - lo
            kcv = b.emit("vector", lambda e, par=par, lo=lo, n=n, dst=dst:
                e.tensor_copy(
                    Kc[0:64, dst:dst + 16 * n].rearrange("p (sg i) -> p sg i", sg=16),
                    Ksb[par][0:64, 0:S_MAIN].rearrange(
                        "p (sg i) -> p sg i", sg=16)[:, :, lo:lo + n]),
                (kfills + [("t", prev_ttop)]) if lo == 240 else [("v", kcv)],
                ("v", 1))
        kcv = b.emit("vector", lambda e, par=par: e.tensor_copy(
            Kc[0:64, 240:255], Ksb[par][0:64, S_MAIN:S_MAIN + 15]),
            [("v", kcv), ("v", khid)], ("v", 1))
        last_kread_v[par] = kcv
        mslot_free = [0, 0]
        tr_prev = 0
        mv = 0
        for qt in range(8):
            r_ = qt % 2
            w = [("v", mslot_free[r_])]
            if qt < 2:
                w.append(("s", last_qkexp.get(r_, 0)))
            if qt == 0:
                w += [(f"gq{par}", qv), ("v", kcv), ("v", prev_tv1)]
            t_top = b.emit("tensor", lambda e, par=par, qt=qt, r_=r_: e.matmul(
                PS[:, r_ * 512:r_ * 512 + 256],
                qa[par][0:64, qt * 128:(qt + 1) * 128],
                Kc[0:64, :], start=True, stop=True), w, ("t", 1))
            prev_ttop = t_top
            tm = b.emit("vector", lambda e, par=par, qt=qt: e.tensor_mul(
                tt[:, :], qa[par][0:64, qt * 128:(qt + 1) * 128],
                ktl[par][0:64, qt * 128:(qt + 1) * 128]),
                [(f"dt{par}", vtv), ("t", tr_prev)] if qt > 0
                else [(f"dt{par}", vtv), (f"gq{par}", qv)], ("v", 1))
            last_kt[par] = tm
            tc = b.emit("tensor", lambda e, qt=qt: e.matmul(
                PS[:, 3328 + qt: 3328 + qt + 1], tt[:, :], o64f[:, :],
                start=True, stop=True), [("v", tm)], ("t", 1))
            tr_prev = b.emit("tensor", lambda e, qt=qt: e.matmul(
                PS[0:1, 2048 + qt * 128: 2048 + (qt + 1) * 128], o64f[:, :],
                tt[:, :], start=True, stop=True),
                [("v", last_epi)] if qt == 0 else (), ("t", 1))
            m1 = b.emit("vector", lambda e, qt=qt, r_=r_: e.tensor_add(
                mt[:, :], PS[:, r_ * 512:r_ * 512 + 256],
                tq[:, qt * 256:(qt + 1) * 256]), [("t", t_top)], ("v", 1))
            m2 = b.emit("vector", lambda e, qt=qt: e.reduce_max(
                mcols[:, qt:qt + 1], mt[:, :], axis=AXX), [("v", m1)], ("v", 1))
            mslot_free[r_] = m2
            m3 = b.emit("vector", lambda e, qt=qt: e.tensor_max(
                mcols[:, qt:qt + 1], mcols[:, qt:qt + 1],
                PS[:, 3328 + qt:3328 + qt + 1]), [("v", m2), ("t", tc)], ("v", 1))
            mv = b.emit("vector", lambda e, qt=qt: e.tensor_scalar(
                mcols[:, qt:qt + 1], mcols[:, qt:qt + 1], 0.08, -1.0,
                op0=ALU.max, op1=ALU.mult), [("v", m3)], ("v", 1))
        tmt = b.emit("tensor", lambda e: e.transpose(
            PS[0:8, 3712:3840], mcols[:, :], iden[:, :]), [("v", mv)], ("t", 1))
        mre = b.emit("vector", lambda e: e.tensor_copy(mrow[:, :], PS[0:8, 3712:3840]),
                     [("t", tmt)], ("v", 1))
        prev_mre = mre
        mrd = b.dma("gpsimd", qa[par][64:65, :].rearrange("p (a c) -> p a c", a=8),
                    mrow[:, :].rearrange("p (o c) -> p o c", o=1), [("v", mre)],
                    f"gq{par}")
        tv1 = b.emit("vector", lambda e, par=par: e.tensor_add(
            trowf[:, :], PS[0:1, 2048:3072], qa[par][64:65, :]),
            [(f"gq{par}", mrd), ("s", prev_se1)], ("v", 1))
        prev_tv1 = tv1
        last_qread_v[par] = tv1
        se1 = b.emit("scalar", lambda e: e.activation(
            out=etail[:, :], in_=trowf[:, :], func=AF.Exp, scale=0.125),
            [("v", tv1), ("t", prev_t_end)], ("s", 1))
        prev_se1 = se1
        for blk in range(2):
            b.emit("tensor", lambda e, blk=blk: e.matmul(
                PS[0:64, 2048 + blk * 512: 2048 + (blk + 1) * 512],
                o1b[0:1, 0:64], etail[0:1, blk * 512:(blk + 1) * 512],
                start=True, stop=True), [("s", se1)] if blk == 0 else ())
        kready = [(f"dK{par}", b.v(f"dK{par}")), (f"dV{par}", b.v(f"dV{par}")),
                  (f"gd{par}", b.v(f"gd{par}")), (f"gq{par}", b.v(f"gq{par}")),
                  ("v", khid), ("v", vhid)]
        epi_h0 = 0
        pbc = 0
        for half in range(2):
            avbank = 1024 + half * 512
            chunks = sched[half]
            for ci, ch in enumerate(chunks):
                r = pbc % 2
                qw = list(kready) if (ci == 0 and half == 0) else []
                if last_qkexp.get(r):
                    qw.append(("s", last_qkexp[r]))
                qk = b.emit("tensor", lambda e, par=par, ch=ch, r=r, half=half:
                    e.matmul(PS[:, r * 512:(r + 1) * 512],
                             Ksb[par][:, ch * 128:(ch + 1) * 128],
                             qa[par][:, half * 512:(half + 1) * 512],
                             start=True, stop=True), qw, ("t", 1))
                last_k[par] = qk
                last_q[par] = qk
                expw = [("t", qk)]
                if ch == 32:
                    bb = b.emit("vector", lambda e, r=r, half=half: e.tensor_add(
                        PS[0:15, r * 512:(r + 1) * 512],
                        PS[0:15, r * 512:(r + 1) * 512],
                        tb[0:15, half * 512:(half + 1) * 512]),
                        [("t", qk)], ("v", 1))
                    expw.append(("v", bb))
                if ch in (33, 34, 35, 36):
                    dmc = 0 if ch in (33, 35) else 1
                    bb = b.emit("vector", lambda e, r=r, dmc=dmc: e.tensor_add(
                        PS[:, r * 512:(r + 1) * 512],
                        PS[:, r * 512:(r + 1) * 512],
                        dm[:, dmc * 512:(dmc + 1) * 512]),
                        [("t", qk), ("dg", init_dg)], ("v", 1))
                    expw.append(("v", bb))
                pb = pbc % 3
                pbc += 1
                if last_avP.get(pb):
                    expw.append(("t", last_avP[pb]))
                ex = b.emit("scalar", lambda e, r=r, pb=pb: e.activation(
                    out=P[pb][:, :], in_=PS[:, r * 512:(r + 1) * 512],
                    func=AF.Exp, scale=0.125), expw, ("s", 1))
                last_qkexp[r] = ex
                avw = [("s", ex)]
                if ci == 0:
                    avw.append(("v", epi_h0 if half == 1 else last_epi))
                av = b.emit("tensor", lambda e, par=par, ch=ch, pb=pb, avbank=avbank,
                            st=(ci == 0), sp=(ci == len(chunks) - 1):
                    e.matmul(PS[0:65, avbank:avbank + 512],
                             Vsb[par][:, ch * 65:(ch + 1) * 65], P[pb][:, :],
                             start=st, stop=sp), avw, ("t", 1))
                last_avP[pb] = av
                last_v[par] = av
            e1 = b.emit("vector", lambda e, par=par, half=half: e.tensor_mul(
                en[:, :], PS[0:64, 2048 + half * 512: 2048 + (half + 1) * 512],
                vtl[par][0:64, half * 512:(half + 1) * 512]),
                [("t", b.v("t")), (f"dt{par}", vtv)], ("v", 1))
            last_vt[par] = e1
            e2 = b.emit("vector", lambda e, avbank=avbank: e.tensor_add(
                en[:, :], en[:, :], PS[0:64, avbank:avbank + 512]),
                [("v", e1)], ("v", 1))
            e3 = b.emit("vector", lambda e, half=half, avbank=avbank: e.tensor_add(
                dro[:, :], PS[64:65, avbank:avbank + 512],
                etail[0:1, half * 512:(half + 1) * 512]),
                [("v", e2), ("t", prev_e4)], ("v", 1))
            denb = 1536 - half * 512
            e4 = b.emit("tensor", lambda e, denb=denb: e.matmul(
                PS[0:64, denb:denb + 512], o1f[0:1, 0:64], dro[0:1, :],
                start=True, stop=True), [("v", e3)], ("t", 1))
            prev_e4 = e4
            if DEBUG_BH0 and bh == 0 and half == 0:
                dst = sb["dbgsta"]
                vv9 = b.emit("vector", lambda e, avbank=avbank: e.tensor_copy(
                    dst[:, 0:512], PS[0:65, avbank:avbank + 512]),
                    [("v", e3)], ("v", 1))
                b.dma("sync", d["dbg_av"][:, :], dst[0:65, 0:512], [("v", vv9)], "do")
                b.dma("sync", d["dbg_dro"][:, :], dro[:, :], [("v", vv9)], "do")
                b.dma("sync", d["dbg_etail"][:, :], etail[:, :], [("v", vv9)], "do")
                b.dma("sync", d["dbg_mcols"][:, :], mcols[:, :], [("v", vv9)], "do")
                b.dma("sync", d["dbg_qa"][:, :], qa[par][:, :], [("v", vv9)], "do")
                b.dma("sync", d["dbg_ksb"][:, :], Ksb[par][:, :], [("v", vv9)], "do")
                b.dma("sync", d["dbg_vsb"][:, :], Vsb[par][:, :], [("v", vv9)], "do")
                b.dma("sync", d["dbg_en"][:, :], en[:, :], [("v", vv9)], "do")
                fin = b.dma("sync", d["dbg_trow"][:, :], trowf[:, :],
                            [("v", vv9)], "do")
                b.emit("sync", lambda e: e.nop(), [("do", fin)])
                return [("do", fin)]
            e5 = b.emit("vector", lambda e, denb=denb: e.reciprocal(
                erc[:, :], PS[0:64, denb:denb + 512]), [("t", e4)], ("v", 1))
            e6 = b.emit("vector", lambda e: e.tensor_mul(atmp[:, :], en[:, :],
                        erc[:, :]), [("v", e5), ("ga", prev_adma)], ("v", 1))
            if half == 0:
                epi_h0 = e6
            adma = b.dma("gpsimd", attn_sb[hh * 64:(hh + 1) * 64,
                                           hp * 2048 + bq * 1024 + half * 512:
                                           hp * 2048 + bq * 1024 + (half + 1) * 512],
                         atmp[:, :], [("v", e6)], "ga")
            prev_adma = adma
        last_epi = e6
        prev_t_end = b.v("t")
    return [("ga", adma), ("v", b.v("v"))]


def _emit_proj(b, sb, PS, d, attn_sb, x_sb, attn_done):
    ring = [sb[f"Fwr{i}"] for i in range(8)]
    ring_free = [b.v("t")] * 8
    prev = 0
    xw = []
    for ot in range(IC):
        i = ot % 8
        w = [("t", ring_free[i])] if ring_free[i] else ()
        dv = b.emit("gpsimd", lambda e, i=i, ot=ot: e.dma_start(
            out=ring[i][:, :].rearrange("p (ic o) -> p ic o", ic=8),
            in_=d["WprojT"][:, ot * 128:(ot + 1) * 128].rearrange(
                "(ic p) o -> p ic o", p=128)), w, (f"w{i}", 16))
        tv = None
        for icx in range(IC):
            for blk in range(4):
                w = []
                if icx == 0 and blk == 0:
                    w = [(f"w{i}", dv), ("v", prev)] + (list(attn_done) if ot == 0 else [])
                tv = b.emit("tensor", lambda e, i=i, icx=icx, blk=blk: e.matmul(
                    PS[:, blk * 512:(blk + 1) * 512],
                    ring[i][:, icx * 128:(icx + 1) * 128],
                    attn_sb[:, icx * 2048 + blk * 512: icx * 2048 + (blk + 1) * 512],
                    start=(icx == 0), stop=(icx == IC - 1)), w,
                    ("t", 1) if (icx == IC - 1 and blk == 3) else None)
        ring_free[i] = tv
        prev = b.emit("vector", lambda e, ot=ot: e.tensor_add(
            x_sb[:, ot * 2048:(ot + 1) * 2048], x_sb[:, ot * 2048:(ot + 1) * 2048],
            PS[:, 0:2048]), [("t", tv)], ("v", 1))
        xw.append(("v", prev))
    return xw[-2:]


def build_nc(stage):
    nc = bass.Bass("TRN2", target_bir_lowering=False, debug=False)
    d = {}
    d["xT"] = nc.dram_tensor("xT", [C, NTOK], F32, kind="ExternalInput").ap()
    for w in ("wfT", "ufT", "wiT", "uiT", "wcT", "ucT", "woT", "uoT", "WprojT"):
        d[w] = nc.dram_tensor(w, [C, C], F32, kind="ExternalInput").ap()
    d["WattnT"] = nc.dram_tensor("WattnT", [C, 3 * C], F32, kind="ExternalInput").ap()
    for v in ("bf", "bi", "bc", "bo", "in_c"):
        d[v] = nc.dram_tensor(v, [C, 1], F32, kind="ExternalInput").ap()
    d["xout"] = nc.dram_tensor("xout", [C, NTOK], F32, kind="ExternalOutput").ap()
    d["iden"] = nc.dram_tensor("iden", [128, 128], F32, kind="ExternalInput").ap()
    cin_t = cga_t = None
    if stage >= 2:
        d["diagmask"] = nc.dram_tensor("diagmask", [128, 1024], F32,
                                       kind="ExternalInput").ap()
        d["killrow"] = nc.dram_tensor("killrow", [2, S_TOT], F32,
                                      kind="ExternalInput").ap()
        d["topbias"] = nc.dram_tensor("topbias", [15, 1024], F32,
                                      kind="ExternalInput").ap()
        d["topbiasqs"] = nc.dram_tensor("topbiasqs", [128, 2048], F32,
                                        kind="ExternalInput").ap()
        d["vcol"] = nc.dram_tensor("vcol", [128, NCH], F32,
                                   kind="ExternalInput").ap()
        d["qkill"] = nc.dram_tensor("qkill", [2, 1024], F32,
                                    kind="ExternalInput").ap()
        d["permat"] = nc.dram_tensor("permat", [16, 16], F32,
                                     kind="ExternalInput").ap()
        if DEBUG_BH0:
            for nm, shp, dt_ in (("dbg_av", [65, 512], F32), ("dbg_dro", [1, 512], F32),
                                 ("dbg_etail", [1, 1024], F32),
                                 ("dbg_mcols", [128, 8], F32),
                                 ("dbg_qa", [KAUG, 1024], BF16),
                                 ("dbg_ksb", [KAUG, S_TOT], BF16),
                                 ("dbg_vsb", [128, NCH * 65], BF16),
                                 ("dbg_en", [64, 512], F32),
                                 ("dbg_trow", [1, 1024], F32)):
                d[nm] = nc.dram_tensor(nm, shp, dt_, kind="ExternalOutput").ap()
        cin_t = nc.dram_tensor("cin", [2, 2, 16, 32768], BF16)
        cga_t = nc.dram_tensor("cga", [8, 2, 2, 16, 32768], BF16)
        d["cin"] = cin_t.ap()
        d["cga"] = cga_t.ap()
        d["tails"] = nc.dram_tensor("tails", [2, 16, 2, 64, 1024], BF16).ap()
        d["q_d"] = nc.dram_tensor("q_d", [8, 128, NTOK], BF16).ap()

    stack = ExitStack()
    b = Bld(nc, stack)
    with stack:
        PS = stack.enter_context(nc.psum_tensor("PS", [128, 4096], F32))
        sb = {}

        def alloc(name, shape, dt, st=None):
            sb[name] = (st or stack).enter_context(nc.sbuf_tensor(name, shape, dt))
            return sb[name]

        x_sb = alloc("x", [128, IC * NTOK], F32)
        y_sb = alloc("y", [128, IC * NTOK], BF16)
        scr = {}
        scr["ones128f"] = alloc("ones128f", [128, 1], F32)
        scr["ones128b"] = alloc("ones128b", [128, 1], BF16)
        scr["ones1f"] = alloc("ones1f", [1, 128], F32)
        scr["eps"] = alloc("epsc", [128, 1], F32)
        iden_sb = alloc("iden_sb", [128, 128], F32)
        sb["ones64f"] = alloc("ones64f", [64, 1], F32)
        sb["ones1b"] = alloc("ones1b", [1, 128], BF16)
        for tagn in ("F",):
            for i in range(8):
                alloc(f"{tagn}wr{i}", [128, 1024], BF16)
            for i in range(3):
                alloc(f"{tagn}g{i}", [128, 1024], BF16)
            alloc(f"{tagn}t0", [128, 1024], BF16)
            alloc(f"{tagn}t1", [128, 1024], BF16)
            alloc(f"{tagn}ucb", [128, 24], F32)
        cols = {}
        for v, cn in (("in_c", "in_f"), ("bf", "bf"), ("bi", "bi"), ("bc", "bc"),
                      ("bo", "bo")):
            cols[cn] = alloc("col_" + cn, [128, IC], F32)
        cols["in_b"] = alloc("col_in_b", [128, IC], BF16)

        def _ln_scope(ph, tagn):
            scr["xsq"] = alloc("xsq" + tagn, [128, IC * NTOK], BF16, ph)
            lnr = alloc("lnrows" + tagn, [1, 3 * NTOK], F32, ph)
            scr["mrow"] = lnr[0:1, 0:NTOK]
            scr["trow"] = lnr[0:1, NTOK:2 * NTOK]
            scr["rrow"] = lnr[0:1, 2 * NTOK:3 * NTOK]
            scr["mrb"] = alloc("mrb" + tagn, [128, 2 * NTOK], F32, ph)

        b.emit("gpsimd", lambda e: e.memset(scr["ones128f"][:, :], 1.0), (), ("g", 1))
        b.emit("gpsimd", lambda e: e.memset(scr["ones128b"][:, :], 1.0), (), ("g", 1))
        b.emit("gpsimd", lambda e: e.memset(scr["ones1f"][:, :], 1.0), (), ("g", 1))
        b.emit("gpsimd", lambda e: e.memset(scr["eps"][:, :], 1e-5), (), ("g", 1))
        b.emit("gpsimd", lambda e: e.memset(sb["ones64f"][:, :], 1.0), (), ("g", 1))
        b.emit("gpsimd", lambda e: e.memset(sb["ones1b"][:, :], 1.0), (), ("g", 1))
        b.dma("sync", iden_sb[:, :], d["iden"][:, :])
        ginit = ("g", b.v("g"))
        xw = []
        for ic in range(IC):
            dv = b.dma("sync", x_sb[:, ic * NTOK:(ic + 1) * NTOK],
                       d["xT"][ic * 128:(ic + 1) * 128, :])
            xw.append(("d", dv))
        for v, cn in (("in_c", "in_f"), ("bf", "bf"), ("bi", "bi"), ("bc", "bc"),
                      ("bo", "bo")):
            src = d[v].rearrange("(ic p) o -> p ic o", p=128)
            b.dma("sync", cols[cn][:, :].rearrange("p (ic o) -> p ic o", o=1),
                  src, slow=True)
        b.dma("gpsimd", cols["in_b"][:, :].rearrange("p (ic o) -> p ic o", o=1),
              d["in_c"].rearrange("(ic p) o -> p ic o", p=128), (), "dg", slow=True)
        loads = [("d", b.v("d")), ("dg", b.v("dg")), ginit]

        W = {"wf": d["wfT"], "uf": d["ufT"], "wi": d["wiT"], "ui": d["uiT"],
             "wc": d["wcT"], "uc": d["ucT"], "wo": d["woT"], "uo": d["uoT"]}

        if stage == 0:
            xdone = loads
        elif stage == -1:
            with ExitStack() as ph:
                _ln_scope(ph, "1")
                yready = _emit_ln(b, sb, PS, x_sb, y_sb, scr, loads, "L1")
            xdone = []
            for ic in range(IC):
                vv = b.emit("vector", lambda e, ic=ic: e.tensor_copy(
                    x_sb[:, ic * NTOK:(ic + 1) * NTOK],
                    y_sb[:, ic * NTOK:(ic + 1) * NTOK]), yready, ("v", 1))
                xdone.append(("v", vv))
            xdone = xdone[-2:]
        if stage >= 1:
            with ExitStack() as ph:
                _ln_scope(ph, "1")
                yready = _emit_ln(b, sb, PS, x_sb, y_sb, scr, loads, "L1")
            with ExitStack() as ph:
                sb["c1"] = alloc("c1", [128, IC * 1024], BF16, ph)
                sb["c2"] = alloc("c2", [128, IC * 1024], BF16, ph)
                xdone = _emit_ffn(b, sb, PS, y_sb, x_sb, W, cols, yready, loads, "F", True)

        if stage in (15, 2, 3):
            with ExitStack() as ph:
                _ln_scope(ph, "2")
                yready = _emit_ln(b, sb, PS, x_sb, y_sb, scr, xdone, "L2")
            with ExitStack() as ph:
                with ExitStack() as ph2:
                    sb["TR4"] = alloc("TR4", [128, 1024], F32, ph2)
                    sb["KT4"] = alloc("KT4", [128, NTOK], BF16, ph2)
                    sb["Kst4"] = alloc("Kst4", [128, 1024], BF16, ph2)
                    sb["Vtr4"] = alloc("Vtr4", [128, 1024], BF16, ph2)
                    sb["kt"] = alloc("kt", [128, NTOK], BF16, ph2)
                    sb["qtile"] = alloc("qtile", [128, NTOK], BF16, ph2)
                    contrib = _emit_qkv(b, sb, PS, d, y_sb, d["q_d"], d["cin"],
                                        d["tails"], yready, iden_sb)
                    if not (QKV_PARTS & 16):
                        contrib = [("t", 0)]
                    agv = b.emit("gpsimd", lambda e: e.collective_compute(
                        "AllGather", ALU.bypass,
                        replica_groups=[list(range(NCORES))],
                        ins=[cin_t.ap()], outs=[cga_t.ap()]), contrib, ("cc", 1))
                    ag_done = [("cc", agv)]
                if stage == 15:
                    xdone = list(ag_done) + xdone
                for nm, shp, dt in (
                        ("Ksb0", [KAUG, S_TOT], BF16), ("Ksb1", [KAUG, S_TOT], BF16),
                        ("Vsb0", [128, NCH * 65], BF16), ("Vsb1", [128, NCH * 65], BF16),
                        ("qa0", [KAUG, 1024], BF16), ("qa1", [KAUG, 1024], BF16),
                        ("ktl0", [64, 1024], BF16), ("ktl1", [64, 1024], BF16),
                        ("vtl0", [64, 1024], BF16), ("vtl1", [64, 1024], BF16),
                        ("P0", [128, 512], BF16), ("P1", [128, 512], BF16),
                        ("P2", [128, 512], BF16),
                        ("khi", [64, 16], F32), ("vhi16", [16, 64], F32),
                        ("vhs", [64, 16], F32), ("vtop", [64, 16], F32),
                        ("mcols", [128, 8], F32), ("mrow8", [8, 128], BF16),
                        ("mt", [128, 256], BF16), ("ttail", [64, 128], F32),
                        ("trowf", [1, 1024], F32), ("etail", [1, 1024], BF16),
                        ("en", [64, 512], F32), ("erc", [64, 512], F32),
                        ("dro", [1, 512], F32), ("atmp", [64, 512], BF16),
                        ("dm", [128, 1024], BF16), ("tb", [15, 1024], BF16),
                        ("tq", [128, 2048], BF16), ("permat_sb", [16, 16], F32),
                        ("Kc", [64, 256], BF16),
                        ("vcol_sb", [128, NCH], BF16)) + (
                        (("dbgsta", [65, 512], F32),) if DEBUG_BH0 else ()):
                    alloc(nm, shp, dt, ph)
                sb["permat"] = sb["permat_sb"]
                b.dma("sync", sb["permat"][:, :], d["permat"][:, :])
                if stage != 15:
                    attn_done = _emit_attn(b, sb, PS, d, d["q_d"], d["cin"],
                                           d["cga"], d["tails"], y_sb, ag_done,
                                           iden_sb)
            if stage != 15:
                xdone = _emit_proj(b, sb, PS, d, y_sb, x_sb, attn_done)

        if stage == 3:
            with ExitStack() as ph:
                _ln_scope(ph, "3")
                yready = _emit_ln(b, sb, PS, x_sb, y_sb, scr, xdone, "L3")
            with ExitStack() as ph:
                sb["c1"] = alloc("c1b", [128, IC * 1024], BF16, ph)
                sb["c2"] = alloc("c2b", [128, IC * 1024], BF16, ph)
                xdone = _emit_ffn(b, sb, PS, y_sb, x_sb, W, cols, yready, loads, "F", False)

        fin = []
        for ic in range(IC):
            dv = b.dma("sync", d["xout"][ic * 128:(ic + 1) * 128, :],
                       x_sb[:, ic * NTOK:(ic + 1) * NTOK], xdone, "do")
            fin.append(dv)
        b.emit("sync", lambda e: e.nop(), [("do", fin[-1])])

        with nc.Block() as block:
            b.install(block)
    return nc


def _slot_tables():
    out = []
    for L in (4, 8, 16, 32, 64, 128, 256, 512):
        for i in range(SPAN // L):
            out.append((L, i, (i + 1) * L - 1))
    return out


_R_TOP = ([1024 * (i + 1) - 1 for i in range(8)] +
          [2048 * (i + 1) - 1 for i in range(4)] +
          [4096 * (i + 1) - 1 for i in range(2)] + [8192 * 1 - 1])


def _host_constants(c):
    slots = _slot_tables()
    own = [c, NSPAN - 1 - c]
    dmv = np.full((128, 1024), -1e9, np.float32)
    for ch in range(2):
        for p in range(128):
            sl = ch * 128 + p
            if sl >= 255:
                continue
            r = slots[sl][2]
            dmv[p, ch * 512 + r: (ch + 1) * 512] = 0.0
    kill = np.zeros((2, S_TOT), np.float32)
    for half in range(2):
        m = own[half]
        for sg in range(NSPAN):
            if sg >= m:
                kill[half, sg * 256:(sg + 1) * 256] = 1.0
        kill[half, S_MAIN + S_HI + (1 - half) * 256:
             S_MAIN + S_HI + (2 - half) * 256] = 1.0
    tbv = np.zeros((15, 1024), np.float32)
    for half in range(2):
        base_t = own[half] * SPAN
        for k in range(15):
            for t in range(SPAN):
                if _R_TOP[k] > base_t + t:
                    tbv[k, half * 512 + t] = -1e9
    r_coarse = np.empty(256, np.int64)
    for col in range(128):
        sg, i = col // 8, col % 8
        r_coarse[col] = sg * SPAN + (i + 1) * 64 - 1
    for col in range(128, 192):
        sg, i = (col - 128) // 4, (col - 128) % 4
        r_coarse[col] = sg * SPAN + (i + 1) * 128 - 1
    for col in range(192, 224):
        sg, i = (col - 192) // 2, (col - 192) % 2
        r_coarse[col] = sg * SPAN + (i + 1) * 256 - 1
    for col in range(224, 240):
        r_coarse[col] = (col - 224) * SPAN + 511
    r_coarse[240:255] = _R_TOP
    r_coarse[255] = 10 ** 9
    tqv = np.zeros((128, 2048), np.float32)
    for qt in range(8):
        half = qt // 4
        for p in range(128):
            t = own[half] * SPAN + (qt % 4) * 128 + p
            tqv[p, qt * 256:(qt + 1) * 256] = np.where(r_coarse > t, -1e9, 0.0)
    vcol = np.ones((128, NCH), np.float32)
    for ch in range(32):
        if ch % 2 == 1:
            vcol[127, ch] = 0.0
    vcol[15:128, 32] = 0.0
    vcol[127, 34] = 0.0
    vcol[127, 36] = 0.0
    qk_ = np.zeros((2, 1024), np.float32)
    qk_[0, 0:512] = -8e9
    qk_[1, 512:1024] = -8e9
    pm = np.zeros((16, 16), np.float32)
    pm[0:8, 0:8] = np.eye(8)
    pm[8:16, 8:16] = np.eye(8)[::-1]
    return {"diagmask": dmv, "killrow": kill, "topbias": tbv, "topbiasqs": tqv,
            "vcol": vcol, "qkill": qk_, "permat": pm}


def _shard_positions(c):
    s0, s1 = c, NSPAN - 1 - c
    return np.concatenate([np.arange(s0 * SPAN, (s0 + 1) * SPAN),
                           np.arange(s1 * SPAN, (s1 + 1) * SPAN)])


def kernel(L0_attn_x, L0_attn_W_attn, L0_attn_W_proj, L0_ffn_wf, L0_ffn_uf, L0_ffn_bf,
           L0_ffn_wi, L0_ffn_ui, L0_ffn_bi, L0_ffn_wc, L0_ffn_uc, L0_ffn_bc,
           L0_ffn_wo, L0_ffn_uo, L0_ffn_bo, L0_ffn_in_c):
    x = np.asarray(L0_attn_x, dtype=np.float32)
    WT = {
        "WattnT": np.ascontiguousarray(np.asarray(L0_attn_W_attn, np.float32).T),
        "WprojT": np.ascontiguousarray(np.asarray(L0_attn_W_proj, np.float32).T),
        "wfT": np.ascontiguousarray(np.asarray(L0_ffn_wf, np.float32).T),
        "ufT": np.ascontiguousarray(np.asarray(L0_ffn_uf, np.float32).T),
        "wiT": np.ascontiguousarray(np.asarray(L0_ffn_wi, np.float32).T),
        "uiT": np.ascontiguousarray(np.asarray(L0_ffn_ui, np.float32).T),
        "wcT": np.ascontiguousarray(np.asarray(L0_ffn_wc, np.float32).T),
        "ucT": np.ascontiguousarray(np.asarray(L0_ffn_uc, np.float32).T),
        "woT": np.ascontiguousarray(np.asarray(L0_ffn_wo, np.float32).T),
        "uoT": np.ascontiguousarray(np.asarray(L0_ffn_uo, np.float32).T),
    }
    vecs = {
        "bf": np.asarray(L0_ffn_bf, np.float32).reshape(C, 1),
        "bi": np.asarray(L0_ffn_bi, np.float32).reshape(C, 1),
        "bc": np.asarray(L0_ffn_bc, np.float32).reshape(C, 1),
        "bo": np.asarray(L0_ffn_bo, np.float32).reshape(C, 1),
        "in_c": np.asarray(L0_ffn_in_c, np.float32).reshape(C, 1),
    }
    nc = build_nc(STAGE)
    iden = np.eye(128, dtype=np.float32)
    in_maps = []
    for c in range(NCORES):
        pos = _shard_positions(c)
        xs = x[:, pos, :]
        xT = np.ascontiguousarray(xs.transpose(2, 0, 1).reshape(C, NTOK))
        m = {"xT": xT, "iden": iden}
        m.update(WT)
        m.update(vecs)
        if STAGE in (15, 2, 3):
            m.update(_host_constants(c))
        in_maps.append(m)
    res = run_bass_kernel_spmd(nc, in_maps, core_ids=list(range(NCORES)))
    out = np.empty((B, T, C), np.float32)
    for c in range(NCORES):
        pos = _shard_positions(c)
        y = res.results[c]["xout"]
        out[:, pos, :] = y.T.reshape(B, NPOS, C)
    return out
